# version 05
# baseline (speedup 1.0000x reference)
import numpy as np
import ml_dtypes
import concourse.bass as bass
import concourse.mybir as mybir
from concourse.bass_utils import run_bass_kernel_spmd

F32 = mybir.dt.float32
BF16 = mybir.dt.bfloat16
F16 = mybir.dt.float16
I16 = mybir.dt.int16
I32 = mybir.dt.int32
AF = mybir.ActivationFunctionType
ALU = mybir.AluOpType
AX = mybir.AxisListType

B_FULL = 16
NCORES = 8
NB_BATCH = B_FULL // NCORES
S_FULL = 2048
D = 1024
H = 4096
C = 256
NB = 128
GT = 256
HC = H // 128
DCK = D // 128
HSW = GT + 2

ENG2SEM = {"tensor": "T", "vector": "V", "scalar": "S", "gpsimd": "G",
           "sync": "Y"}


class Emit:

    def __init__(self, which, eng, sems, strict=False, limit=None):
        self.which = which
        self.eng = eng
        self.sems = sems
        self.n = {k: 0 for k in sems}
        self.waited = {}
        self.strict = strict
        self.ownsem = ENG2SEM.get(which)
        self.limit = limit
        self.total = 0
        self.tser = False

    def op(self, engine, fn, waits=(), inc=None, amt=1, serialize=False):
        if inc is None:
            inc = ENG2SEM[engine]
        self.total += 1
        if self.limit is not None and self.total > self.limit:
            self.n[inc] += amt
            return self.n[inc]
        if engine == "tensor" and (serialize or self.tser):
            if engine == self.which:
                v0 = self.n["T"]
                if v0 > 0 and v0 > self.waited.get("T", 0):
                    self.eng.wait_ge(self.sems["T"], v0)
                    self.waited["T"] = v0
            self.tser = serialize
        if engine == self.which:
            if (self.strict and self.ownsem is not None
                    and self.which != "tensor"):
                v0 = self.n.get(self.ownsem, 0)
                if v0 > 0 and v0 > self.waited.get(self.ownsem, 0):
                    self.eng.wait_ge(self.sems[self.ownsem], v0)
                    self.waited[self.ownsem] = v0
            for sk, v in waits:
                if v > 0 and v > self.waited.get(sk, 0):
                    self.eng.wait_ge(self.sems[sk], v)
                    self.waited[sk] = v
            inst = fn(self.eng)
            inst.then_inc(self.sems[inc], amt)
        self.n[inc] += amt
        return self.n[inc]

    def op_noinc(self, engine, fn, waits=()):
        if engine == self.which:
            for sk, v in waits:
                if v > 0 and v > self.waited.get(sk, 0):
                    self.eng.wait_ge(self.sems[sk], v)
                    self.waited[sk] = v
            fn(self.eng)

    def barrier_wait(self, engine, waits):
        if self.limit is not None:
            return
        if engine == self.which:
            for sk, v in waits:
                if v > 0 and v > self.waited.get(sk, 0):
                    self.eng.wait_ge(self.sems[sk], v)
                    self.waited[sk] = v


def host_consts():
    i = np.arange(128)
    return dict(
        emat=np.maximum(i[:, None] - i[None, :] - 1, 0).astype(np.float32),
        mmat=(i[:, None] > i[None, :]).astype(np.float32),
        eyef=np.eye(128, dtype=np.float32),
        e32=(i[:, None] % 32 == np.arange(32)[None, :]).astype(np.float32),
        onesf=np.ones((1, 128), np.float32),
        onem=np.ones((128, 128), np.float32),
        onec=np.ones((128, 1), np.float32),
        qe=(127.0 - i)[:, None].astype(np.float32),
        se=i[:, None].astype(np.float32),
    )


DBG_SKIP = set()


def build_nc(S=S_FULL, nbatch=NB_BATCH, strict=True, detect_races=False, limit=None):
    NG = S // GT
    ROWS = nbatch * S
    nc = bass.Bass("TRN2", target_bir_lowering=False, debug=False,
                   detect_race_conditions=detect_races)

    din = {}
    for name, shape in [
            ("x", [ROWS, D]), ("basis", [D, NB]), ("basisT", [NB, D]),
            ("qcT", [NB, C]), ("kcT", [NB, C]), ("vcT", [NB, C]),
            ("oc", [C, NB]), ("w1t", [D, H]), ("w2t", [H, D]),
            ("convw", [H, 3]), ("convb", [H]), ("dlogit", [1, 1]),
            ("oscale", [1, 1])] + [(k, list(v.shape))
                                   for k, v in host_consts().items()]:
        din[name] = nc.dram_tensor(name, shape, F32, kind="ExternalInput").ap()
    out_ap = nc.dram_tensor("out", [ROWS, D], F32, kind="ExternalOutput").ap()
    dbg_ap = None
    if "DBGTAP" in DBG_SKIP:
        dbg_ap = nc.dram_tensor("dbg", [128, 64], F32,
                                kind="ExternalOutput").ap()
        dbgb_ap = nc.dram_tensor("dbgb", [128, 1024], mybir.dt.bfloat16,
                                 kind="ExternalOutput").ap()
        dbgh_ap = nc.dram_tensor("dbgh", [128, 512], F16,
                                 kind="ExternalOutput").ap()
    from contextlib import ExitStack
    ctx = ExitStack()

    def sbuf(name, shape, dtype):
        return ctx.enter_context(nc.sbuf_tensor("sb_" + name, shape, dtype))

    def psum(name, shape):
        return ctx.enter_context(nc.psum_tensor(name, shape, F32))

    w1q = sbuf("w1q", [128, DCK * H], BF16)
    w2q = sbuf("w2q", [128, HC * D], F16)
    basis_bf = sbuf("basis_bf", [128, DCK * NB], BF16)
    basisT_bf = sbuf("basisT_bf", [128, D], BF16)
    qcT_bf = sbuf("qcT_bf", [128, C], BF16)
    kcT_bf = sbuf("kcT_bf", [128, C], BF16)
    vcT_bf = sbuf("vcT_bf", [128, C], BF16)
    oc_bf = sbuf("oc_bf", [128, 2 * NB], BF16)
    convdiag = sbuf("convdiag", [128, 96 * 32], F16)
    convb_sb = sbuf("convb_sb", [128, HC], F32)
    eyef = sbuf("eyef", [128, 128], F32)
    eye_bf = sbuf("eye_bf", [128, 128], BF16)
    eye_h = sbuf("eye_h", [128, 128], F16)
    onem = sbuf("onem", [128, 128], F32)
    onesf = onem[0:1, 0:128]
    dgt = sbuf("dgt", [128, 128], F32)
    convw_sb = dgt[:, 0:96]
    e32_sb = dgt[:, 96:128]
    onec = sbuf("onec", [128, 1], F32)
    dtmask = sbuf("dtmask", [128, 128], F32)
    scal = dtmask[0:1, 120:128]
    cols = sbuf("cols", [128, 30], F32)
    hs0 = sbuf("hs0", [128, HC * HSW + 2], F16)
    hs1 = sbuf("hs1", [128, HC * HSW + 2], F16)
    hsb = [hs0, hs1]
    xg0 = sbuf("xg0", [128, 2 * D], F32)
    xg1 = sbuf("xg1", [128, 2 * D], F32)
    xgb = [xg0, xg1]
    xqsT = sbuf("xqsT", [128, DCK * GT], BF16)
    xb_sb = sbuf("xb_sb", [128, 128], BF16)
    QT_sb = sbuf("QT_sb", [128, C], BF16)
    KT_sb = sbuf("KT_sb", [128, C], BF16)
    V_sb = sbuf("V_sb", [128, C], BF16)
    amask = sbuf("amask", [128, 128], BF16)
    Rg0 = sbuf("Rg0", [128, 2 * C], BF16)
    Rg1 = sbuf("Rg1", [128, 2 * C], BF16)
    Rgb = [Rg0, Rg1]
    u_sb = sbuf("u_sb", [128, 128], BF16)
    S_sb = sbuf("S_sb", [128, 2 * C], BF16)
    scr = sbuf("scr", [128, 512], F16)
    scol_sb = sbuf("scol_sb", [128, GT], F16)
    g_sb = sbuf("g_sb", [128, D], F32)
    xn_f = g_sb
    xbT_sb = xb_sb
    Ktm = KT_sb
    uT_sb = u_sb

    scrf = sbuf("scrf", [128, 512], F32)
    scr_f32 = scrf.ap()

    psA = psum("psA", [128, 512])
    psB = psum("psB", [128, 512])
    psAB = [psA, psB]
    psC = psum("psC", [128, 512])
    psD = psum("psD", [128, 512])
    psCD = [psC, psD]
    psE = psum("psE", [128, 512])
    psF = psum("psF", [128, 512])
    psG = psum("psG", [128, 512])
    psH = psum("psH", [128, 512])

    sems = {}
    for name in ["T", "V", "S", "G", "Y", "DC", "DW0", "DW1", "DX0", "DX1",
                 "DO0", "DO1"]:
        sems[name] = ctx.enter_context(nc.semaphore(name))

    LND, DCN, OSC, QFC, DVC, W1M, W2M, SW1, SW2 = range(9)
    MU, SQT, VAR, HS_, HT_, RSTD, NMR, M1C, R1S = 9, 10, 11, 12, 13, 14, 15, 16, 17
    M2C, RBC, SCL0, SCL1 = 18, 19, 20, 21

    def col(i):
        return cols[:, i:i + 1]

    def schedule(E: Emit):
        n = E.n
        res = {}

        def use(key):
            return [res[key]] if key in res else []

        def setr(key, semname):
            res[key] = (semname, n[semname])

        def htouch(b, semname):
            res["hs%d_%s" % (b, semname)] = (semname, n[semname])

        def hwaits(b):
            return [res[k] for k in
                    ("hs%d_T" % b, "hs%d_V" % b, "hs%d_S" % b) if k in res]

        psgrot = [0]

        def psg_slot():
            return 0

        def psg_slot2():
            if psgrot[0] % 2 == 1:
                psg_slot()
            sl = psgrot[0]
            psg_slot()
            psg_slot()
            return sl


        def transp(src_ap, bf, waits=()):
            sl = psg_slot()
            ident = eye_h if bf == "f16" else (eye_bf if bf else eyef)
            E.op("tensor", lambda t, s=sl, a=src_ap, idn=ident:
                 t.matmul(out=psG[:, s * 128:(s + 1) * 128], lhsT=a,
                          rhs=idn[:, :], start=True, stop=True),
                 waits=use("psG%d" % sl) + list(waits))
            return sl

        def hap(kb, c0, nch, w=GT, off=1):
            a = hsb[kb][:, c0 * HSW + off: c0 * HSW + off + nch * HSW]
            return a.rearrange("p (a q) -> p a q", a=nch, q=HSW)[:, :, 0:w]

        loads = [
            (eyef[:, :], din["eyef"]),
            (e32_sb, din["e32"]),
            (onem[:, :], din["onem"]),
            (onec[:, :], din["onec"]),
            (cols[:, 9:10], din["qe"]),
            (cols[:, 10:11], din["se"]),
            (scal[0:1, 0:1], din["dlogit"]),
            (scal[0:1, 1:2], din["oscale"]),
            (xg0[:, 0:128], din["emat"]),
            (xg0[:, 128:256], din["mmat"]),
            (xg0[:, 256:256 + C], din["qcT"]),
            (xg0[:, 512:512 + C], din["kcT"]),
            (xg0[:, 768:768 + C], din["vcT"]),
            (xg0[:, 1024:1024 + 2 * NB].rearrange(
                "p (c n) -> p c n", c=2),
             din["oc"].rearrange("(c p) n -> p c n", p=128)),
            (xg1[:, 0:DCK * NB].rearrange("p (c n) -> p c n", c=DCK),
             din["basis"].rearrange("(c p) n -> p c n", p=128)),
            (xg1[:, D:2 * D], din["basisT"]),
            (convw_sb[:, :].rearrange("p (c k) -> p c k", c=HC),
             din["convw"].rearrange("(c p) k -> p c k", p=128)),
            (convb_sb[:, :], din["convb"].rearrange("(c p) -> p c", p=128)),
        ]
        for dst, src in loads:
            E.op("sync", lambda y, d=dst, s=src: y.dma_start(
                out=d, in_=s, allow_slow_non_contiguous=True),
                inc="DC", amt=16)
        dc_all = n["DC"]

        casts = [
            (basis_bf[:, :], xg1[:, 0:DCK * NB]),
            (basisT_bf[:, :], xg1[:, D:2 * D]),
            (qcT_bf[:, :], xg0[:, 256:256 + C]),
            (kcT_bf[:, :], xg0[:, 512:512 + C]),
            (vcT_bf[:, :], xg0[:, 768:768 + C]),
            (oc_bf[:, :], xg0[:, 1024:1024 + 2 * NB]),
            (eye_bf[:, :], eyef[:, :]),
            (eye_h[:, :], eyef[:, :]),
        ]
        for dst, src in casts:
            E.op("vector", lambda v, d=dst, s=src: v.tensor_copy(out=d, in_=s),
                 waits=[("DC", dc_all)])
        if "convdiag" not in DBG_SKIP:
            for idx in range(96):
                E.op("vector", lambda v, i=idx: v.tensor_scalar(
                    out=convdiag[:, i * 32:(i + 1) * 32], in0=e32_sb,
                    scalar1=convw_sb[:, i:i + 1], scalar2=None, op0=ALU.mult),
                    waits=[("DC", dc_all)])

        E.op("vector", lambda v: v.tensor_scalar(
            out=col(28), in0=onec[:, :], scalar1=2.0 ** -30,
            scalar2=1.0, op0=ALU.add, op1=ALU.subtract),
            waits=[("DC", dc_all)])
        E.op("vector", lambda v: v.tensor_scalar(
            out=col(28), in0=col(28), scalar1=2.0 ** 30, scalar2=None,
            op0=ALU.mult))
        E.op("vector", lambda v: v.tensor_scalar(
            out=col(29), in0=col(28), scalar1=-1.0, scalar2=1.0,
            op0=ALU.mult, op1=ALU.add))
        E.op("scalar", lambda s: s.activation(
            out=scal[0:1, 2:3], in_=scal[0:1, 0:1], func=AF.Sigmoid),
            waits=[("DC", dc_all)])
        E.op("scalar", lambda s: s.activation(
            out=scal[0:1, 3:4], in_=scal[0:1, 2:3], func=AF.Ln))
        E.op("scalar", lambda s: s.activation(
            out=scal[0:1, 4:5], in_=scal[0:1, 3:4], func=AF.Exp, scale=128.0))
        s_decay = n["S"]

        for src, slot in [(scal[0:1, 3:4], LND), (scal[0:1, 4:5], DCN),
                          (scal[0:1, 1:2], OSC)]:
            E.op("tensor", lambda t, s2=src, sl=slot: t.matmul(
                out=psG[:, sl:sl + 1], lhsT=onesf[:, :], rhs=s2,
                start=True, stop=True), waits=[("S", s_decay), ("DC", dc_all)])
        t_bc = n["T"]
        for slot in (LND, DCN, OSC):
            E.op("vector", lambda v, sl=slot: v.tensor_copy(
                out=col(sl), in_=psG[:, sl:sl + 1]), waits=[("T", t_bc)])
        v_bc = n["V"]
        setr("psG0", "V")

        E.op("vector", lambda v: v.tensor_scalar(
            out=basisT_bf[:, :], in0=basisT_bf[:, :], scalar1=col(OSC),
            scalar2=None, op0=ALU.mult), waits=[("V", v_bc)])
        E.op("scalar", lambda s: s.activation(
            out=col(QFC), in_=cols[:, 9:10], func=AF.Exp, scale=col(LND)),
            waits=[("V", v_bc)])
        E.op("scalar", lambda s: s.activation(
            out=col(DVC), in_=cols[:, 10:11], func=AF.Exp, scale=col(LND)))
        E.op("scalar", lambda s: s.activation(
            out=dtmask[:, :], in_=xg0[:, 0:128], func=AF.Exp, scale=col(LND)))
        s_exp = n["S"]
        E.op("vector", lambda v: v.tensor_tensor(
            out=dtmask[:, :], in0=dtmask[:, :], in1=xg0[:, 128:256],
            op=ALU.mult), waits=[("S", s_exp)])
        setr("xg0", "V")
        setr("xg1", "V")

        wviews = [xg0.ap(), xg1.ap()]

        def wquant(w_apname, nchunk, width, wq_sb, mean_slot, mc_slot, sw_slot,
                   nelem):
            w_ap = din[w_apname]
            hw_ = width // 2
            E.op("vector", lambda v: v.memset(col(MU), 0.0), waits=use("colsA"))
            nhalf = nchunk * 2
            for ci in range(nhalf):
                dw = "DW%d" % (ci % 2)
                row = ci // 2
                off = (ci % 2) * hw_
                E.op("sync", lambda y, r2=row, o2=off, h2=hw_, c=ci:
                     y.dma_start(
                         out=wviews[c % 2][:, 0:h2],
                         in_=w_ap[r2 * 128:(r2 + 1) * 128, o2:o2 + h2]),
                     waits=use("xg%d" % (ci % 2)), inc=dw, amt=16)
                nseg = hw_ // 128
                E.op("vector", lambda v, c=ci, h2=hw_, a=nseg: v.reduce_sum(
                    out=scrf[:, 64:64 + a],
                    in_=wviews[c % 2][:, 0:h2].rearrange(
                        "p (a q) -> p a q", a=a, q=128),
                    axis=AX.X, apply_absolute_value=True),
                    waits=[(dw, n[dw])] + use("scrf"))
                E.op("vector", lambda v, a=nseg: v.reduce_sum(
                    out=col(SQT), in_=scrf[:, 64:64 + a], axis=AX.X))
                setr("scrf", "V")
                E.op("vector", lambda v: v.tensor_tensor(
                    out=col(MU), in0=col(MU), in1=col(SQT), op=ALU.add))
                setr("xg%d" % (ci % 2), "V")
            vsum = n["V"]
            E.op("tensor", lambda t: t.matmul(
                out=psG[:, 0:1], lhsT=onem[:, :], rhs=col(MU),
                start=True, stop=True), waits=[("V", vsum)] + use("psG0"))
            tsum = n["T"]
            E.op("vector", lambda v: v.tensor_scalar(
                out=col(mc_slot), in0=psG[:, 0:1],
                scalar1=1.0 / nelem, scalar2=1e-5, op0=ALU.mult, op1=ALU.max),
                waits=[("T", tsum)])
            E.op("vector", lambda v: v.reciprocal(
                out=col(sw_slot), in_=col(mc_slot)))
            vmc = n["V"]
            setr("psG0", "V")
            for ci in range(nhalf):
                dw = "DW%d" % (ci % 2)
                row = ci // 2
                off = (ci % 2) * hw_
                E.op("sync", lambda y, r2=row, o2=off, h2=hw_, c=ci:
                     y.dma_start(
                         out=wviews[c % 2][:, 0:h2],
                         in_=w_ap[r2 * 128:(r2 + 1) * 128, o2:o2 + h2]),
                     waits=use("xg%d" % (ci % 2)) + [("V", vmc)],
                     inc=dw, amt=16)
                fv = wviews[ci % 2][:, 0:hw_]
                E.op("vector", lambda v, f=fv: v.tensor_scalar(
                    out=f, in0=f, scalar1=col(sw_slot), scalar2=None,
                    op0=ALU.mult), waits=[(dw, n[dw])])
                for sub in range(hw_ // 512):
                    sl_ = slice(sub * 512, (sub + 1) * 512)
                    E.op("vector", lambda v, f=fv, s3=sl_: v.tensor_scalar(
                        out=scrf[:, 0:512], in0=f[:, s3], scalar1=12582912.0,
                        scalar2=12582912.0, op0=ALU.add, op1=ALU.subtract),
                        waits=use("scrf"))
                    E.op("vector", lambda v, f=fv, s3=sl_: v.tensor_scalar(
                        out=f[:, s3], in0=f[:, s3],
                        scalar1=6755399441055744.0,
                        scalar2=6755399441055744.0,
                        op0=ALU.add, op1=ALU.subtract))
                    E.op("vector", lambda v, f=fv, s3=sl_:
                         v.scalar_tensor_tensor(
                             out=f[:, s3], in0=scrf[:, 0:512], scalar=col(29),
                             in1=f[:, s3], op0=ALU.mult, op1=ALU.add))
                    setr("scrf", "V")
                E.op("vector", lambda v, f=fv: v.tensor_scalar(
                    out=f, in0=f, scalar1=-1.0, scalar2=None, op0=ALU.max))
                E.op("vector", lambda v, f=fv, r2=row, o2=off, h2=hw_:
                     v.tensor_scalar(
                         out=wq_sb[:, r2 * width + o2:r2 * width + o2 + h2],
                         in0=f, scalar1=1.0, scalar2=None, op0=ALU.min))
                setr("xg%d" % (ci % 2), "V")

        if "wq" not in DBG_SKIP:
            wquant("w1t", DCK, H, w1q, 5, W1M, SW1, float(D * H))
            wquant("w2t", HC, D, w2q, 6, W2M, SW2, float(H * D))

        def stageA(bi, g, tl, base):
            cur = g % 2
            xs = xgb[cur][:, tl * D:(tl + 1) * D]
            dxk = "DX%d" % cur
            xw = [(dxk, n[dxk])]
            E.op("vector", lambda v: v.reduce_sum(
                out=col(MU), in_=xs, axis=AX.X), waits=xw + use("colsA"))
            E.op("vector", lambda v: v.tensor_scalar(
                out=col(MU), in0=col(MU), scalar1=1.0 / D, scalar2=None,
                op0=ALU.mult))
            E.op("scalar", lambda s: s.activation(
                out=xn_f[:, :], in_=xs, func=AF.Square),
                waits=xw + use("gx"))
            ssq = n["S"]
            setr("gx", "S")
            E.op("vector", lambda v: v.reduce_sum(
                out=col(SQT), in_=xn_f[:, :], axis=AX.X), waits=[("S", ssq)])
            E.op("vector", lambda v: v.tensor_scalar(
                out=col(SQT), in0=col(SQT), scalar1=1.0 / D, scalar2=None,
                op0=ALU.mult))
            E.op("vector", lambda v: v.tensor_tensor(
                out=col(VAR), in0=col(MU), in1=col(MU), op=ALU.mult))
            E.op("vector", lambda v: v.tensor_tensor(
                out=col(VAR), in0=col(SQT), in1=col(VAR), op=ALU.subtract))
            E.op("vector", lambda v: v.tensor_scalar(
                out=col(VAR), in0=col(VAR), scalar1=1e-5, scalar2=None,
                op0=ALU.add))
            E.op("vector", lambda v: v.tensor_scalar(
                out=col(HS_), in0=col(VAR), scalar1=1.0, scalar2=0.5,
                op0=ALU.add, op1=ALU.mult))
            for _ in range(4):
                E.op("vector", lambda v: v.reciprocal(
                    out=col(HT_), in_=col(HS_)))
                E.op("vector", lambda v: v.tensor_tensor(
                    out=col(HT_), in0=col(VAR), in1=col(HT_), op=ALU.mult))
                E.op("vector", lambda v: v.tensor_tensor(
                    out=col(HS_), in0=col(HS_), in1=col(HT_), op=ALU.add))
                E.op("vector", lambda v: v.tensor_scalar(
                    out=col(HS_), in0=col(HS_), scalar1=0.5, scalar2=None,
                    op0=ALU.mult))
            E.op("vector", lambda v: v.reciprocal(out=col(RSTD), in_=col(HS_)))
            E.op("vector", lambda v: v.tensor_tensor(
                out=col(NMR), in0=col(MU), in1=col(RSTD), op=ALU.mult))
            E.op("vector", lambda v: v.tensor_scalar(
                out=col(NMR), in0=col(NMR), scalar1=-1.0, scalar2=None,
                op0=ALU.mult))
            vln = n["V"]
            E.op("scalar", lambda s: s.activation(
                out=xn_f[:, :], in_=xs, func=AF.Identity, bias=col(NMR),
                scale=col(RSTD)), waits=[("V", vln)])
            sxn = n["S"]
            setr("colsA", "S")
            setr("gx", "S")
            E.op("vector", lambda v: v.reduce_max(
                out=col(M1C), in_=xn_f[:, :], axis=AX.X,
                apply_absolute_value=True), waits=[("S", sxn)])
            E.op("vector", lambda v: v.tensor_scalar(
                out=col(M1C), in0=col(M1C), scalar1=1e-5, scalar2=None,
                op0=ALU.max))
            E.op("vector", lambda v: v.reciprocal(out=col(R1S), in_=col(M1C)))
            E.op("vector", lambda v: v.tensor_scalar(
                out=col(R1S), in0=col(R1S), scalar1=127.0, scalar2=None,
                op0=ALU.mult))
            vr1 = n["V"]
            wxnT = use("xqsT") + [res.get("xqsT_t%d" % tl, ("V", 0))]
            for i in range(DCK):
                sl = transp(xn_f[:, i * 128:(i + 1) * 128], bf=False,
                            waits=[("S", sxn)])
                tc = n["T"]
                E.op("vector", lambda v, s2=sl, i2=i, t2=tl: v.tensor_copy(
                    out=xqsT[:, i2 * GT + t2 * 128: i2 * GT + (t2 + 1) * 128],
                    in_=psG[:, s2 * 128:(s2 + 1) * 128]),
                    waits=[("T", tc)] + (wxnT if i == 0 else []))
                setr("psG%d" % sl, "V")
            vxnT = n["V"]
            res["xnf_rd"] = ("T", n["T"])
            w = use("psH0") + [("V", vxnT)]
            for i in range(DCK):
                E.op("tensor", lambda t, i2=i, t2=tl: t.matmul(
                    out=psH[:, 0:128],
                    lhsT=xqsT[:, i2 * GT + t2 * 128: i2 * GT + (t2 + 1) * 128],
                    rhs=basis_bf[:, i2 * NB:(i2 + 1) * NB],
                    start=(i2 == 0), stop=(i2 == DCK - 1)),
                    waits=(w if i == 0 else []))
            txb = n["T"]
            res["xnT_rd"] = ("T", txb)
            E.op("vector", lambda v: v.tensor_copy(
                out=xb_sb[:, :], in_=psH[:, 0:128]),
                waits=[("T", txb)] + use("xb_sb"))
            setr("psH0", "V")
            E.op("vector", lambda v: v.tensor_scalar(
                out=xn_f[:, :], in0=xn_f[:, :], scalar1=col(R1S),
                scalar2=None, op0=ALU.mult),
                waits=[("V", vr1)] + use("xnf_rd"))
            for hh in range(2):
                hsl = slice(hh * 512, (hh + 1) * 512)
                E.op("vector", lambda v, h3=hsl: v.tensor_scalar(
                    out=scr_f32[:, 0:512], in0=xn_f[:, h3],
                    scalar1=12582912.0, scalar2=12582912.0,
                    op0=ALU.add, op1=ALU.subtract), waits=use("scrf"))
                E.op("vector", lambda v, h3=hsl: v.tensor_scalar(
                    out=xn_f[:, h3], in0=xn_f[:, h3], scalar1=6755399441055744.0,
                    scalar2=6755399441055744.0, op0=ALU.add, op1=ALU.subtract))
                E.op("vector", lambda v, h3=hsl: v.scalar_tensor_tensor(
                    out=xn_f[:, h3], in0=scr_f32[:, 0:512], scalar=col(29),
                    in1=xn_f[:, h3], op0=ALU.mult, op1=ALU.add))
            E.op("vector", lambda v, t2=tl: v.tensor_scalar(
                out=col(22 + t2), in0=col(M1C), scalar1=col(W1M + 0),
                scalar2=1.0 / 127.0, op0=ALU.mult, op1=ALU.mult))
            setr("scrf", "V")
            vxqs = n["V"]
            for i in range(DCK):
                sl = transp(xn_f[:, i * 128:(i + 1) * 128], bf=False,
                            waits=[("V", vxqs)])
                tc = n["T"]
                E.op("vector", lambda v, s2=sl, i2=i, t2=tl: v.tensor_copy(
                    out=xqsT[:, i2 * GT + t2 * 128: i2 * GT + (t2 + 1) * 128],
                    in_=psG[:, s2 * 128:(s2 + 1) * 128]),
                    waits=[("T", tc)] +
                    ([res.get("xnT_rd", ("T", 0))] if i == 0 else []))
                setr("psG%d" % sl, "V")
            res["gx"] = ("T", n["T"])
            setr("xqsT_t%d" % tl, "V")
            vxb = n["V"]
            sl = transp(xb_sb[:, :], bf=True, waits=[("V", vxb)])
            ttr = n["T"]
            res["xb_sb"] = ("T", ttr)
            E.op("vector", lambda v, s2=sl: v.tensor_copy(
                out=xbT_sb[:, :], in_=psG[:, s2 * 128:(s2 + 1) * 128]),
                waits=[("T", ttr)] + use("xbT_sb"))
            vxbt = n["V"]
            setr("psG%d" % sl, "V")
            w = (use("psH1") + use("psH2") + [("V", vxbt)])
            for cc in range(2):
                E.op("tensor", lambda t, c2=cc: t.matmul(
                    out=psH[:, 128 + c2 * 128:256 + c2 * 128],
                    lhsT=qcT_bf[:, c2 * 128:(c2 + 1) * 128], rhs=xbT_sb[:, :],
                    start=True, stop=True), waits=(w if cc == 0 else []))
            E.op("tensor", lambda t: t.matmul(
                out=psH[:, 384:512], lhsT=kcT_bf[:, 0:128], rhs=xbT_sb[:, :],
                start=True, stop=True), waits=use("psH3"))
            kslot = psg_slot()
            E.op("tensor", lambda t, s2=kslot: t.matmul(
                out=psG[:, s2 * 128:(s2 + 1) * 128], lhsT=kcT_bf[:, 128:256],
                rhs=xbT_sb[:, :], start=True, stop=True),
                waits=use("psG%d" % kslot))
            E.op("tensor", lambda t: t.matmul(
                out=psE[:, 0:256], lhsT=xbT_sb[:, :], rhs=vcT_bf[:, :],
                start=True, stop=True), waits=use("psE0") + use("psE1h"))
            tqkv = n["T"]
            res["xbT_sb"] = ("T", tqkv)
            E.op("vector", lambda v: v.tensor_copy(
                out=QT_sb[:, :], in_=psH[:, 128:384]),
                waits=[("T", tqkv)] + use("QT_sb"))
            E.op("vector", lambda v: v.tensor_copy(
                out=KT_sb[:, 0:128], in_=psH[:, 384:512]),
                waits=use("KT_sb"))
            E.op("vector", lambda v, s2=kslot: v.tensor_copy(
                out=KT_sb[:, 128:256], in_=psG[:, s2 * 128:(s2 + 1) * 128]))
            setr("psG%d" % kslot, "V")
            setr("psH1", "V")
            setr("psH2", "V")
            setr("psH3", "V")
            setr("QT_sb", "V")
            setr("KT_sb", "V")
            E.op("scalar", lambda s: s.activation(
                out=V_sb[:, :], in_=psE[:, 0:256], func=AF.Identity),
                waits=[("T", tqkv)] + use("V_sb"))
            setr("psE0", "S")
            setr("V_sb", "S")

        def stageB(bi, g, tl, base):
            vq = res.get("QT_sb", ("V", 0))
            vk = res.get("KT_sb", ("V", 0))
            sv = res.get("V_sb", ("S", 0))
            w = use("psH0") + [vk, vq]
            for cc in range(2):
                E.op("tensor", lambda t, c2=cc: t.matmul(
                    out=psH[:, 0:128],
                    lhsT=KT_sb[:, c2 * 128:(c2 + 1) * 128],
                    rhs=QT_sb[:, c2 * 128:(c2 + 1) * 128],
                    start=(c2 == 0), stop=(c2 == 1)),
                    waits=(w if cc == 0 else []))
            ta = n["T"]
            E.op("vector", lambda v: v.tensor_tensor(
                out=amask[:, :], in0=psH[:, 0:128], in1=dtmask[:, :],
                op=ALU.mult), waits=[("T", ta)] + use("amask"))
            vam = n["V"]
            setr("psH0", "V")
            w = use("psE0") + [("V", vam), sv]
            E.op("tensor", lambda t: t.matmul(
                out=psE[:, 0:256], lhsT=amask[:, :], rhs=V_sb[:, :],
                start=True, stop=True), waits=w)
            res["amask"] = ("T", n["T"])
            w = use("psE1h") + use("S_sb")
            for cc in range(2):
                E.op("tensor", lambda t, c2=cc: t.matmul(
                    out=psE[:, 256:512],
                    lhsT=QT_sb[:, c2 * 128:(c2 + 1) * 128],
                    rhs=S_sb[:, c2 * 256:(c2 + 1) * 256],
                    start=(c2 == 0), stop=(c2 == 1)),
                    waits=(w if cc == 0 else []))
            tr = n["T"]
            res["QT_sb"] = ("T", tr)
            Rg = Rgb[g % 2]
            rk = "Rg%d_t%d" % (g % 2, tl)
            E.op("vector", lambda v, t2=tl, R=Rg: v.tensor_copy(
                out=R[:, t2 * C:(t2 + 1) * C], in_=psE[:, 0:256]),
                waits=[("T", tr)] + use(rk))
            E.op("vector", lambda v, t2=tl, R=Rg: v.scalar_tensor_tensor(
                out=R[:, t2 * C:(t2 + 1) * C], in0=psE[:, 256:512],
                scalar=col(QFC), in1=R[:, t2 * C:(t2 + 1) * C],
                op0=ALU.mult, op1=ALU.add))
            setr(rk, "V")
            setr("psE0", "V")
            setr("psE1h", "V")
            for i in range(2):
                sl = transp(KT_sb[:, i * 128:(i + 1) * 128], bf=True,
                            waits=[vk])
                tk = n["T"]
                E.op("scalar", lambda s, s2=sl, i2=i: s.activation(
                    out=Ktm[:, i2 * 128:(i2 + 1) * 128],
                    in_=psG[:, s2 * 128:(s2 + 1) * 128], func=AF.Identity,
                    scale=col(DVC)), waits=[("T", tk)] + use("Ktm"))
                setr("psG%d" % sl, "S")
            res["KT_sb"] = ("T", tk)
            skt = n["S"]
            setr("Ktm", "S")
            w = use("psC0") + use("psC1") + [("S", skt), sv]
            for cc in range(2):
                E.op("tensor", lambda t, c2=cc: t.matmul(
                    out=psC[:, c2 * 256:(c2 + 1) * 256],
                    lhsT=Ktm[:, c2 * 128:(c2 + 1) * 128], rhs=V_sb[:, :],
                    start=True, stop=True), waits=(w if cc == 0 else []))
            tkv = n["T"]
            res["Ktm"] = ("T", tkv)
            res["KT_sb"] = ("T", tkv)
            res["V_sb"] = ("T", tkv)
            E.op("vector", lambda v: v.scalar_tensor_tensor(
                out=S_sb[:, :], in0=S_sb[:, :], scalar=col(DCN),
                in1=psC[:, :], op0=ALU.mult, op1=ALU.add),
                waits=[("T", tkv)])
            setr("S_sb", "V")
            setr("psC0", "V")
            setr("psC1", "V")

        def stageC(g):
            cur = g % 2
            wfirst = ([res.get("xqsT_t0", ("V", 0)),
                       res.get("xqsT_t1", ("V", 0))] + hwaits(cur))
            wev = hwaits(cur)
            for hc in range(HC):
                pb = (hc // 2) % 2
                slot = (hc % 2) * 256
                w = use("psAB%d" % pb) + (wfirst if hc == 0 else [])
                for i in range(DCK):
                    E.op("tensor",
                         lambda t, i2=i, h2=hc, p2=pb, s2=slot: t.matmul(
                             out=psAB[p2][:, s2:s2 + 256],
                             lhsT=w1q[:, i2 * H + h2 * 128:
                                      i2 * H + (h2 + 1) * 128],
                             rhs=xqsT[:, i2 * GT:(i2 + 1) * GT],
                             start=(i2 == 0), stop=(i2 == DCK - 1)),
                         waits=(w if i == 0 else []))
                tmm = n["T"]
                if hc == 0:
                    htouch(cur, "T")
                if hc % 2 == 1:
                    pr = hc // 2
                    dsl = hap(cur, hc - 1, 2)
                    ssl = psAB[pb][:, 0:512].rearrange(
                        "p (a q) -> p a q", a=2, q=256)
                    scb = scol_sb.ap().unsqueeze(1).broadcast_to(
                        [128, 2, 256])
                    E.op("vector", lambda v, d=dsl, s3=ssl, s4=scb:
                         v.tensor_tensor(out=d, in0=s3, in1=s4, op=ALU.mult),
                         waits=[("T", tmm),
                                res.get("scol_copy", ("V", 0))] +
                         (wev if pr == 0 else []))
                    setr("psAB%d" % pb, "V")
            res["xqsT"] = ("T", n["T"])
            setr("scol_sb", "V")
            setr("psH2", "V")
            setr("psH3", "V")
            htouch(cur, "V")
            htouch(cur, "S")

        def tail(bi, k, base):
            kb = k % 2
            hk = hsb[kb]
            wh = hwaits(kb)
            for hc in range(HC):
                pb = hc % 2
                w = use("psCD%d" % pb) + (wh if hc == 0 else [])
                for i in range(4):
                    for kk in range(3):
                        E.op("tensor",
                             lambda t, h2=hc, k2=kk, i2=i, p2=pb:
                             t.matmul(
                                 out=psCD[p2][32 * i2:32 * (i2 + 1), 0:256],
                                 lhsT=convdiag[32 * i2:32 * (i2 + 1),
                                               (h2 * 3 + k2) * 32:
                                               (h2 * 3 + k2 + 1) * 32],
                                 rhs=hk[32 * i2:32 * (i2 + 1),
                                        h2 * HSW + k2:h2 * HSW + k2 + 256],
                                 start=(k2 == 0), stop=(k2 == 2),
                                 tile_position=(32 * i2, 32 * i2)),
                             waits=(w if (kk == 0 and i == 0) else []))
                tcv = n["T"]
                E.op("scalar", lambda s, h2=hc, p2=pb: s.activation(
                    out=hk[:, h2 * HSW + 1:h2 * HSW + 1 + GT],
                    in_=psCD[p2][:, 0:256], func=AF.Silu,
                    bias=convb_sb[:, h2:h2 + 1]),
                    waits=[("T", tcv)] + (wh if hc <= 1 else []))
                setr("psCD%d" % pb, "S")
            htouch(kb, "T")
            sconv = n["S"]
            htouch(kb, "S")
            wtree = [("S", sconv)] + use("scr") + use("scr_rt")
            for op2, base2 in ((ALU.max, 0), (ALU.min, 1)):
                E.op("vector", lambda v, o3=op2: v.tensor_tensor(
                    out=scr[:, 0:512].rearrange("p (a q) -> p a q", a=2, q=GT),
                    in0=hap(kb, 0, 2), in1=hap(kb, 2, 2), op=o3),
                    waits=(wtree if base2 == 0 else []))
                for c0 in range(4, HC, 2):
                    E.op("vector", lambda v, c2=c0, o3=op2: v.tensor_tensor(
                        out=scr[:, 0:512].rearrange(
                            "p (a q) -> p a q", a=2, q=GT),
                        in0=scr[:, 0:512].rearrange(
                            "p (a q) -> p a q", a=2, q=GT),
                        in1=hap(kb, c2, 2), op=o3))
                E.op("vector", lambda v, o3=op2: v.tensor_tensor(
                    out=scr[:, 0:256], in0=scr[:, 0:256], in1=scr[:, 256:512],
                    op=o3))
                vtree = n["V"]
                for tl in range(2):
                    sl = transp(scr[:, tl * 128:(tl + 1) * 128], bf="f16",
                                waits=[("V", vtree)])
                    ttr = n["T"]
                    if base2 == 1 and tl == 1:
                        res["scr_rt"] = ("T", ttr)
                    if base2 == 0:
                        E.op("vector", lambda v, s2=sl, t2=tl: v.reduce_max(
                            out=col(24 + t2),
                            in_=psG[:, s2 * 128:(s2 + 1) * 128], axis=AX.X),
                            waits=[("T", ttr)] + use("colsB"))
                    else:
                        E.op("vector", lambda v, s2=sl, t2=tl: v.tensor_reduce(
                            out=col(M2C),
                            in_=psG[:, s2 * 128:(s2 + 1) * 128], axis=AX.X,
                            op=ALU.min), waits=[("T", ttr)] + use("colsB"))
                        E.op("vector", lambda v, t2=tl: v.tensor_scalar(
                            out=col(M2C), in0=col(M2C), scalar1=-1.0,
                            scalar2=None, op0=ALU.mult))
                        E.op("vector", lambda v, t2=tl: v.tensor_tensor(
                            out=col(M2C), in0=col(M2C), in1=col(24 + t2),
                            op=ALU.max))
                        E.op("vector", lambda v: v.tensor_scalar(
                            out=col(M2C), in0=col(M2C), scalar1=1e-5,
                            scalar2=None, op0=ALU.max))
                        E.op("vector", lambda v, t2=tl: v.tensor_scalar(
                            out=col(SCL0 + t2), in0=col(M2C),
                            scalar1=col(W2M + 0), scalar2=1.0 / 127.0,
                            op0=ALU.mult, op1=ALU.mult))
                        E.op("vector", lambda v, t2=tl: v.reciprocal(
                            out=col(26 + t2), in_=col(M2C)))
                        E.op("vector", lambda v, t2=tl: v.tensor_scalar(
                            out=col(26 + t2), in0=col(26 + t2), scalar1=127.0,
                            scalar2=None, op0=ALU.mult))
                    setr("psG%d" % sl, "V")
                if base2 == 0:
                    setr("scr", "V")
            setr("colsB", "V")
            vrbcols = n["V"]
            slb = psg_slot2()
            for t2 in range(2):
                E.op("vector", lambda v, t3=t2: v.tensor_scalar(
                    out=dgt[:, :], in0=eyef[:, :], scalar1=col(26 + t3),
                    scalar2=None, op0=ALU.mult),
                    waits=[("V", vrbcols)] + use("dgt"))
                vdg2 = n["V"]
                E.op("tensor", lambda t, t3=t2, s2=slb: t.matmul(
                    out=psG[:, (s2 + t3) * 128:(s2 + t3 + 1) * 128],
                    lhsT=onem[:, :], rhs=dgt[:, :], start=True, stop=True),
                    waits=[("V", vdg2)] + use("psG%d" % (slb + t2)))
                res["dgt"] = ("T", n["T"])
            tbc = n["T"]
            rb_ps = psG[:, slb * 128:slb * 128 + 256]
            wrd = [("T", tbc)] + use("scrf") + use("gx")
            for q in range(HC // 2):
                c0 = q * 2
                E.op("vector", lambda v, c2=c0, rp=rb_ps: v.tensor_tensor(
                    out=scr_f32[:, 0:512].rearrange(
                        "p (a q2) -> p a q2", a=2, q2=GT),
                    in0=hap(kb, c2, 2),
                    in1=rp.unsqueeze(1).broadcast_to([128, 2, GT]),
                    op=ALU.mult), waits=(wrd if q == 0 else []))
                vq1 = n["V"]
                E.op("gpsimd", lambda g: g.tensor_scalar(
                    out=g_sb[:, 0:512], in0=scr_f32[:, 0:512],
                    scalar1=12582912.0, scalar2=12582912.0,
                    op0=ALU.add, op1=ALU.subtract), waits=[("V", vq1)])
                E.op("gpsimd", lambda g: g.tensor_scalar(
                    out=scr_f32[:, 0:512], in0=scr_f32[:, 0:512],
                    scalar1=6755399441055744.0, scalar2=6755399441055744.0,
                    op0=ALU.add, op1=ALU.subtract))
                gq = n["G"]
                E.op("vector", lambda v, c2=c0, gq2=gq: v.scalar_tensor_tensor(
                    out=hap(kb, c2, 2),
                    in0=g_sb[:, 0:512].rearrange(
                        "p (a q2) -> p a q2", a=2, q2=GT),
                    scalar=col(29),
                    in1=scr_f32[:, 0:512].rearrange(
                        "p (a q2) -> p a q2", a=2, q2=GT),
                    op0=ALU.mult, op1=ALU.add), waits=[("G", gq)])
            res["gx"] = ("V", n["V"])
            setr("scrf", "V")
            vhq = n["V"]
            setr("psG%d" % slb, "V")
            setr("psG%d" % (slb + 1), "V")
            htouch(kb, "V")
            setr("scr", "V")
            Rg = Rgb[kb]
            xk = xgb[kb]
            dxk = "DX%d" % kb
            for tl in range(2):
                w = (use("psE0") + use("psE1h") + use("psF") + [("V", vhq)] +
                     [res.get("colsB", ("V", 0))])
                for hc in range(HC):
                    for dgrp, p2 in ((0, psE), (1, psF)):
                        E.op("tensor",
                             lambda t, h2=hc, d2=dgrp, t2=tl, p3=p2: t.matmul(
                                 out=p3[:, 0:512],
                                 lhsT=hk[:, h2 * HSW + 1 + t2 * 128:
                                         h2 * HSW + 1 + (t2 + 1) * 128],
                                 rhs=w2q[:, h2 * D + d2 * 512:
                                         h2 * D + (d2 + 1) * 512],
                                 start=(h2 == 0), stop=(h2 == HC - 1)),
                             waits=(w if (hc == 0 and dgrp == 0) else []))
                tf2 = n["T"]
                htouch(kb, "T")
                for dgrp, p2 in ((0, psE), (1, psF)):
                    E.op("scalar", lambda s, d2=dgrp, p3=p2, t2=tl:
                         s.activation(
                             out=g_sb[:, d2 * 512:(d2 + 1) * 512],
                             in_=p3[:, 0:512], func=AF.Silu,
                             scale=col(SCL0 + t2)),
                         waits=[("T", tf2), ("V", vhq)] +
                         (use("gx") if dgrp == 0 else []))
                sg_ = n["S"]
                setr("gx", "S")
                res["colsB"] = ("S", sg_)
                setr("psE0", "S")
                setr("psE1h", "S")
                setr("psF", "S")
                rgw = [res.get("Rg%d_t%d" % (kb, tl), ("V", 0))]
                for i in range(2):
                    sl = transp(Rg[:, tl * C + i * 128:tl * C + (i + 1) * 128],
                                bf=True, waits=rgw)
                    trt = n["T"]
                    E.op("vector", lambda v, s2=sl, i2=i, t2=tl, R=Rg:
                         v.tensor_copy(
                             out=R[:, t2 * C + i2 * 128:
                                   t2 * C + (i2 + 1) * 128],
                             in_=psG[:, s2 * 128:(s2 + 1) * 128]),
                         waits=[("T", trt)])
                    setr("psG%d" % sl, "V")
                vrt = n["V"]
                slu = psg_slot()
                w = use("psG%d" % slu) + [("V", vrt)]
                for cc in range(2):
                    E.op("tensor", lambda t, c2=cc, s2=slu, t2=tl, R=Rg:
                         t.matmul(
                             out=psG[:, s2 * 128:(s2 + 1) * 128],
                             lhsT=R[:, t2 * C + c2 * 128:
                                    t2 * C + (c2 + 1) * 128],
                             rhs=oc_bf[:, c2 * 128:(c2 + 1) * 128],
                             start=(c2 == 0), stop=(c2 == 1)),
                         waits=(w if cc == 0 else []))
                tu = n["T"]
                res["Rg%d_t%d" % (kb, tl)] = ("T", tu)
                E.op("vector", lambda v, s2=slu: v.tensor_copy(
                    out=u_sb[:, :], in_=psG[:, s2 * 128:(s2 + 1) * 128]),
                    waits=[("T", tu)] + use("u_sb"))
                setr("psG%d" % slu, "V")
                vu_ = n["V"]
                slut = transp(u_sb[:, :], bf=True, waits=[("V", vu_)])
                tut = n["T"]
                res["u_sb"] = ("T", tut)
                E.op("vector", lambda v, s2=slut: v.tensor_copy(
                    out=uT_sb[:, :], in_=psG[:, s2 * 128:(s2 + 1) * 128]),
                    waits=[("T", tut)] + use("uT_sb"))
                vut = n["V"]
                setr("psG%d" % slut, "V")
                w = use("psE0") + use("psE1h") + use("psF") + [("V", vut)]
                for dgrp, p2 in ((0, psE), (1, psF)):
                    E.op("tensor", lambda t, d2=dgrp, p3=p2: t.matmul(
                        out=p3[:, 0:512], lhsT=uT_sb[:, :],
                        rhs=basisT_bf[:, d2 * 512:(d2 + 1) * 512],
                        start=True, stop=True), waits=(w if dgrp == 0 else []))
                ts_ = n["T"]
                res["uT_sb"] = ("T", ts_)
                E.op("vector", lambda v: v.tensor_tensor(
                    out=g_sb[:, 0:512], in0=g_sb[:, 0:512], in1=psE[:, 0:512],
                    op=ALU.mult), waits=[("T", ts_), ("S", sg_)])
                E.op("vector", lambda v: v.tensor_tensor(
                    out=g_sb[:, 512:1024], in0=g_sb[:, 512:1024],
                    in1=psF[:, 0:512], op=ALU.mult))
                setr("psE0", "V")
                setr("psE1h", "V")
                setr("psF", "V")
                E.op("vector", lambda v, t2=tl, xk2=xk: v.tensor_tensor(
                    out=xk2[:, t2 * D:(t2 + 1) * D],
                    in0=xk2[:, t2 * D:(t2 + 1) * D], in1=g_sb[:, :],
                    op=ALU.add), waits=[(dxk, n[dxk])])
                vcb = n["V"]
                setr("gx", "V")
                r0 = base + k * GT + tl * 128
                do = "DO%d" % kb
                E.op("gpsimd", lambda gp, r=r0, t2=tl, xk2=xk: gp.dma_start(
                    out=out_ap[r:r + 128, :], in_=xk2[:, t2 * D:(t2 + 1) * D]),
                    waits=[("V", vcb)], inc=do, amt=16)
            res["xg%d" % kb] = ("DO%d" % kb, n["DO%d" % kb])

        for bi in range(nbatch):
            base = bi * S
            E.op("vector", lambda v: v.memset(S_sb[:, :], 0.0),
                 waits=use("S_sb"))
            setr("S_sb", "V")
            hi = (NG - 1) % 2
            E.op("vector", lambda v, b=hi: v.memset(
                hsb[b][:, GT + 1:HC * HSW:HSW], 0.0), waits=hwaits(hi))
            htouch(hi, "V")
            for g0 in (NG - 1, NG - 2):
                if g0 < 0:
                    continue
                bidx = g0 % 2
                dx = "DX%d" % bidx
                r0 = base + g0 * GT
                E.op("sync", lambda y, b=bidx, r=r0: y.dma_start(
                    out=xgb[b].ap().rearrange("p (a d) -> p a d", a=2),
                    in_=din["x"][r:r + GT, :].rearrange(
                        "(a p) d -> p a d", p=128)),
                    waits=use("xg%d" % bidx), inc=dx, amt=16)
                setr("xg%d" % bidx, dx)

            for g in range(NG - 1, -1, -1):
                cur = g % 2
                for tl in (1, 0):
                    if "A" not in DBG_SKIP:
                        stageA(bi, g, tl, base)
                    if "B" not in DBG_SKIP:
                        stageB(bi, g, tl, base)
                if dbg_ap is not None and g == NG - 1 and bi == 0:
                    wv = [("V", n["V"]), ("S", n["S"]), ("T", n["T"])]
                    E.op("gpsimd", lambda gp: gp.dma_start(
                        out=dbg_ap[:, 0:10], in_=cols[:, 0:10]), waits=wv,
                        inc="DO0", amt=16)
                    E.op("gpsimd", lambda gp: gp.dma_start(
                        out=dbgb_ap[:, 0:256], in_=basis_bf[:, 0:256]),
                        inc="DO0", amt=16)
                    E.op("gpsimd", lambda gp: gp.dma_start(
                        out=dbgb_ap[:, 256:512], in_=xqsT[:, 0:256]),
                        inc="DO0", amt=16)
                    E.op("gpsimd", lambda gp: gp.dma_start(
                        out=dbgb_ap[:, 512:768], in_=QT_sb[:, 0:256]),
                        inc="DO0", amt=16)
                    E.op("gpsimd", lambda gp: gp.dma_start(
                        out=dbgb_ap[:, 768:1024], in_=S_sb[:, 0:256]),
                        inc="DO0", amt=16)
                    E.op("gpsimd", lambda gp: gp.dma_start(
                        out=dbgh_ap[:, 0:512], in_=w2q[:, 0:512]),
                        inc="DO0", amt=16)
                if "SCOL" not in DBG_SKIP:
                    vsc = n["V"]
                    for tl in range(2):
                        E.op("vector", lambda v, t2=tl: v.tensor_scalar(
                            out=dgt[:, :], in0=eyef[:, :],
                            scalar1=col(22 + t2), scalar2=None, op0=ALU.mult),
                            waits=use("dgt"))
                        vdg = n["V"]
                        E.op("tensor", lambda t, t2=tl: t.matmul(
                            out=psH[:, 256 + t2 * 128:384 + t2 * 128],
                            lhsT=onem[:, :], rhs=dgt[:, :],
                            start=True, stop=True),
                            waits=[("V", vdg)] +
                            (use("psH2") + use("psH3") if tl == 0 else []))
                        res["dgt"] = ("T", n["T"])
                    tscb = n["T"]
                    E.op("vector", lambda v: v.tensor_copy(
                        out=scol_sb[:, :], in_=psH[:, 256:512]),
                        waits=[("T", tscb)] + use("scol_sb"))
                    vscb = n["V"]
                    res["scol_copy"] = ("V", vscb)
                if "C" not in DBG_SKIP:
                    stageC(g)
                if g <= NG - 2:
                    oth = (g + 1) % 2
                    wf = hwaits(cur) + hwaits(oth)
                    E.op("vector", lambda v, o=oth, c=cur: v.tensor_copy(
                        out=hsb[o][:, 0:HC * HSW:HSW],
                        in_=hsb[c][:, GT:HC * HSW:HSW]), waits=wf)
                    E.op("vector", lambda v, o=oth, c=cur: v.tensor_copy(
                        out=hsb[c][:, GT + 1:HC * HSW:HSW],
                        in_=hsb[o][:, 1:HC * HSW:HSW]))
                    htouch(cur, "V")
                    htouch(oth, "V")
                    if "TAIL" not in DBG_SKIP:
                        tail(bi, g + 1, base)
                if 1 <= g <= NG - 2:
                    gnext = g - 1
                    bidx = gnext % 2
                    dx = "DX%d" % bidx
                    r0 = base + gnext * GT
                    E.op("sync", lambda y, b=bidx, r=r0: y.dma_start(
                        out=xgb[b].ap().rearrange("p (a d) -> p a d", a=2),
                        in_=din["x"][r:r + GT, :].rearrange(
                            "(a p) d -> p a d", p=128)),
                        waits=use("xg%d" % bidx), inc=dx, amt=16)
                    setr("xg%d" % bidx, dx)
            E.op("vector", lambda v: v.memset(hs0[:, 0:HC * HSW:HSW], 0.0),
                 waits=hwaits(0))
            htouch(0, "V")
            if "TAIL" not in DBG_SKIP:
                tail(bi, 0, base)

        E.barrier_wait("gpsimd", [("DO0", n["DO0"]), ("DO1", n["DO1"])])

    with ctx:
        with nc.Block() as blk:
            for engname, reg in (("sync", blk.sync), ("tensor", blk.tensor),
                                 ("vector", blk.vector),
                                 ("scalar", blk.scalar),
                                 ("gpsimd", blk.gpsimd)):
                def body(eng, engname=engname):
                    schedule(Emit(engname, eng, sems, strict=strict,
                                  limit=limit))
                reg(body)
    return nc


def kernel(L0_attn_x, L0_attn_basis, L0_attn_q_coeffs, L0_attn_k_coeffs,
           L0_attn_v_coeffs, L0_attn_o_coeffs, L0_attn_decay_logit,
           L0_attn_out_scale, L0_ffn_w1, L0_ffn_conv_w, L0_ffn_conv_b,
           L0_ffn_w2):
    x = np.ascontiguousarray(np.asarray(L0_attn_x, dtype=np.float32))
    basis = np.ascontiguousarray(np.asarray(L0_attn_basis, np.float32))
    com = dict(
        basis=basis,
        basisT=np.ascontiguousarray(basis.T),
        qcT=np.ascontiguousarray(np.asarray(L0_attn_q_coeffs, np.float32).T),
        kcT=np.ascontiguousarray(np.asarray(L0_attn_k_coeffs, np.float32).T),
        vcT=np.ascontiguousarray(np.asarray(L0_attn_v_coeffs, np.float32).T),
        oc=np.ascontiguousarray(np.asarray(L0_attn_o_coeffs, np.float32)),
        w1t=np.ascontiguousarray(np.asarray(L0_ffn_w1, np.float32).T),
        w2t=np.ascontiguousarray(np.asarray(L0_ffn_w2, np.float32).T),
        convw=np.ascontiguousarray(
            np.asarray(L0_ffn_conv_w, np.float32).reshape(H, 3)),
        convb=np.ascontiguousarray(np.asarray(L0_ffn_conv_b, np.float32)),
        dlogit=np.asarray(L0_attn_decay_logit, np.float32).reshape(1, 1),
        oscale=np.asarray(L0_attn_out_scale, np.float32).reshape(1, 1),
    )
    com.update(host_consts())
    nc = build_nc()
    in_maps = []
    for c in range(NCORES):
        shard = np.ascontiguousarray(
            x[c * NB_BATCH:(c + 1) * NB_BATCH].reshape(NB_BATCH * S_FULL, D))
        m = {"x": shard}
        m.update(com)
        in_maps.append(m)
    res = run_bass_kernel_spmd(nc, in_maps, core_ids=list(range(NCORES)))
    outs = [np.asarray(r["out"], np.float32).reshape(NB_BATCH, S_FULL, D)
            for r in res.results]
    return np.concatenate(outs, axis=0)


# version 06 (final)
# speedup vs baseline: 1.6928x; 1.6928x over previous
import numpy as np
import ml_dtypes
import concourse.bass as bass
import concourse.mybir as mybir
from concourse.bass_utils import run_bass_kernel_spmd

F32 = mybir.dt.float32
BF16 = mybir.dt.bfloat16
F16 = mybir.dt.float16
I16 = mybir.dt.int16
I32 = mybir.dt.int32
AF = mybir.ActivationFunctionType
ALU = mybir.AluOpType
AX = mybir.AxisListType

B_FULL = 16
NCORES = 8
NB_BATCH = B_FULL // NCORES
S_FULL = 2048
D = 1024
H = 4096
C = 256
NB = 128
GT = 256
HC = H // 128
DCK = D // 128
HSW = GT + 2

ENG2SEM = {"tensor": "T", "vector": "V", "scalar": "S", "gpsimd": "G",
           "sync": "Y"}


class Emit:

    def __init__(self, which, eng, sems, strict=False, limit=None):
        self.which = which
        self.eng = eng
        self.sems = sems
        self.n = {k: 0 for k in sems}
        self.waited = {}
        self.strict = strict
        self.ownsem = ENG2SEM.get(which)
        self.limit = limit
        self.total = 0
        self.tser = False

    def op(self, engine, fn, waits=(), inc=None, amt=1, serialize=False,
           selfwait=True):
        if inc is None:
            inc = ENG2SEM[engine]
        self.total += 1
        if self.limit is not None and self.total > self.limit:
            self.n[inc] += amt
            return self.n[inc]
        if engine == "tensor" and (serialize or self.tser):
            if engine == self.which:
                v0 = self.n["T"]
                if v0 > 0 and v0 > self.waited.get("T", 0):
                    self.eng.wait_ge(self.sems["T"], v0)
                    self.waited["T"] = v0
            self.tser = serialize
        if engine == self.which:
            if (self.strict and selfwait and self.ownsem is not None
                    and self.which != "tensor"):
                v0 = self.n.get(self.ownsem, 0)
                if v0 > 0 and v0 > self.waited.get(self.ownsem, 0):
                    self.eng.wait_ge(self.sems[self.ownsem], v0)
                    self.waited[self.ownsem] = v0
            for sk, v in waits:
                if v > 0 and v > self.waited.get(sk, 0):
                    self.eng.wait_ge(self.sems[sk], v)
                    self.waited[sk] = v
            inst = fn(self.eng)
            inst.then_inc(self.sems[inc], amt)
        self.n[inc] += amt
        return self.n[inc]

    def op_noinc(self, engine, fn, waits=()):
        if engine == self.which:
            for sk, v in waits:
                if v > 0 and v > self.waited.get(sk, 0):
                    self.eng.wait_ge(self.sems[sk], v)
                    self.waited[sk] = v
            fn(self.eng)

    def barrier_wait(self, engine, waits):
        if self.limit is not None:
            return
        if engine == self.which:
            for sk, v in waits:
                if v > 0 and v > self.waited.get(sk, 0):
                    self.eng.wait_ge(self.sems[sk], v)
                    self.waited[sk] = v


def host_consts():
    i = np.arange(128)
    return dict(
        emat=np.maximum(i[:, None] - i[None, :] - 1, 0).astype(np.float32),
        mmat=(i[:, None] > i[None, :]).astype(np.float32),
        eyef=np.eye(128, dtype=np.float32),
        e32=(i[:, None] % 32 == np.arange(32)[None, :]).astype(np.float32),
        onesf=np.ones((1, 128), np.float32),
        onem=np.ones((128, 128), np.float32),
        onec=np.ones((128, 1), np.float32),
        qe=(127.0 - i)[:, None].astype(np.float32),
        se=i[:, None].astype(np.float32),
    )


DBG_SKIP = set()


def build_nc(S=S_FULL, nbatch=NB_BATCH, strict=True, detect_races=False, limit=None):
    NG = S // GT
    ROWS = nbatch * S
    nc = bass.Bass("TRN2", target_bir_lowering=False, debug=False,
                   detect_race_conditions=detect_races)

    din = {}
    for name, shape in [
            ("x", [ROWS, D]), ("basis", [D, NB]), ("basisT", [NB, D]),
            ("qcT", [NB, C]), ("kcT", [NB, C]), ("vcT", [NB, C]),
            ("oc", [C, NB]), ("w1t", [D, H]), ("w2t", [H, D]),
            ("convw", [H, 3]), ("convb", [H]), ("dlogit", [1, 1]),
            ("oscale", [1, 1])] + [(k, list(v.shape))
                                   for k, v in host_consts().items()]:
        din[name] = nc.dram_tensor(name, shape, F32, kind="ExternalInput").ap()
    out_ap = nc.dram_tensor("out", [ROWS, D], F32, kind="ExternalOutput").ap()
    dbg_ap = None
    if "DBGTAP" in DBG_SKIP:
        dbg_ap = nc.dram_tensor("dbg", [128, 64], F32,
                                kind="ExternalOutput").ap()
        dbgb_ap = nc.dram_tensor("dbgb", [128, 1024], mybir.dt.bfloat16,
                                 kind="ExternalOutput").ap()
        dbgh_ap = nc.dram_tensor("dbgh", [128, 512], F16,
                                 kind="ExternalOutput").ap()
    from contextlib import ExitStack
    ctx = ExitStack()

    def sbuf(name, shape, dtype):
        return ctx.enter_context(nc.sbuf_tensor("sb_" + name, shape, dtype))

    def psum(name, shape):
        return ctx.enter_context(nc.psum_tensor(name, shape, F32))

    w1q = sbuf("w1q", [128, DCK * H], BF16)
    w2q = sbuf("w2q", [128, HC * D], F16)
    basis_bf = sbuf("basis_bf", [128, DCK * NB], BF16)
    basisT_bf = sbuf("basisT_bf", [128, D], BF16)
    qcT_bf = sbuf("qcT_bf", [128, C], BF16)
    kcT_bf = sbuf("kcT_bf", [128, C], BF16)
    vcT_bf = sbuf("vcT_bf", [128, C], BF16)
    oc_bf = sbuf("oc_bf", [128, 2 * NB], BF16)
    convdiag = sbuf("convdiag", [128, 96 * 32], F16)
    convb_sb = sbuf("convb_sb", [128, HC], F32)
    eyef = sbuf("eyef", [128, 128], F32)
    eye_bf = sbuf("eye_bf", [128, 128], BF16)
    eye_h = sbuf("eye_h", [128, 128], F16)
    onem = sbuf("onem", [128, 128], F32)
    onesf = onem[0:1, 0:128]
    dgt = sbuf("dgt", [128, 128], F32)
    convw_sb = dgt[:, 0:96]
    e32_sb = dgt[:, 96:128]
    onec = sbuf("onec", [128, 1], F32)
    dtmask = sbuf("dtmask", [128, 128], F32)
    scal = dtmask[0:1, 120:128]
    cols = sbuf("cols", [128, 30], F32)
    hs0 = sbuf("hs0", [128, HC * HSW + 2], F16)
    hs1 = sbuf("hs1", [128, HC * HSW + 2], F16)
    hsb = [hs0, hs1]
    xg0 = sbuf("xg0", [128, 2 * D], F32)
    xg1 = sbuf("xg1", [128, 2 * D], F32)
    xgb = [xg0, xg1]
    xqsT = sbuf("xqsT", [128, DCK * GT], BF16)
    xb_sb = sbuf("xb_sb", [128, 128], BF16)
    QT_sb = sbuf("QT_sb", [128, C], BF16)
    KT_sb = sbuf("KT_sb", [128, C], BF16)
    V_sb = sbuf("V_sb", [128, C], BF16)
    amask = sbuf("amask", [128, 128], BF16)
    Rg0 = sbuf("Rg0", [128, 2 * C], BF16)
    Rg1 = sbuf("Rg1", [128, 2 * C], BF16)
    Rgb = [Rg0, Rg1]
    u_sb = sbuf("u_sb", [128, 128], BF16)
    S_sb = sbuf("S_sb", [128, 2 * C], BF16)
    scr = sbuf("scr", [128, 512], F16)
    scol_sb = sbuf("scol_sb", [128, GT], F16)
    g_sb = sbuf("g_sb", [128, D], F32)
    xn_f = g_sb
    xbT_sb = xb_sb
    Ktm = KT_sb
    uT_sb = u_sb

    scrf = sbuf("scrf", [128, 512], F32)
    scr_f32 = scrf.ap()

    psA = psum("psA", [128, 512])
    psB = psum("psB", [128, 512])
    psAB = [psA, psB]
    psC = psum("psC", [128, 512])
    psD = psum("psD", [128, 512])
    psCD = [psC, psD]
    psE = psum("psE", [128, 512])
    psF = psum("psF", [128, 512])
    psG = psum("psG", [128, 512])
    psH = psum("psH", [128, 512])

    sems = {}
    for name in ["T", "V", "S", "G", "Y", "DC", "DW0", "DW1", "DX0", "DX1",
                 "DO0", "DO1"]:
        sems[name] = ctx.enter_context(nc.semaphore(name))

    LND, DCN, OSC, QFC, DVC, W1M, W2M, SW1, SW2 = range(9)
    MU, SQT, VAR, HS_, HT_, RSTD, NMR, M1C, R1S = 9, 10, 11, 12, 13, 14, 15, 16, 17
    M2C, RBC, SCL0, SCL1 = 18, 19, 20, 21

    def col(i):
        return cols[:, i:i + 1]

    def schedule(E: Emit):
        n = E.n
        res = {}

        def use(key):
            return [res[key]] if key in res else []

        def setr(key, semname):
            res[key] = (semname, n[semname])

        def htouch(b, semname):
            res["hs%d_%s" % (b, semname)] = (semname, n[semname])

        def hwaits(b):
            return [res[k] for k in
                    ("hs%d_T" % b, "hs%d_V" % b, "hs%d_S" % b) if k in res]

        psgrot = [0]

        def psg_slot():
            return 0

        def psg_slot2():
            if psgrot[0] % 2 == 1:
                psg_slot()
            sl = psgrot[0]
            psg_slot()
            psg_slot()
            return sl


        def transp(src_ap, bf, waits=()):
            sl = psg_slot()
            ident = eye_h if bf == "f16" else (eye_bf if bf else eyef)
            E.op("tensor", lambda t, s=sl, a=src_ap, idn=ident:
                 t.matmul(out=psG[:, s * 128:(s + 1) * 128], lhsT=a,
                          rhs=idn[:, :], start=True, stop=True),
                 waits=use("psG%d" % sl) + list(waits))
            return sl

        def hap(kb, c0, nch, w=GT, off=1):
            a = hsb[kb][:, c0 * HSW + off: c0 * HSW + off + nch * HSW]
            return a.rearrange("p (a q) -> p a q", a=nch, q=HSW)[:, :, 0:w]

        loads = [
            (eyef[:, :], din["eyef"]),
            (e32_sb, din["e32"]),
            (onem[:, :], din["onem"]),
            (onec[:, :], din["onec"]),
            (cols[:, 9:10], din["qe"]),
            (cols[:, 10:11], din["se"]),
            (scal[0:1, 0:1], din["dlogit"]),
            (scal[0:1, 1:2], din["oscale"]),
            (xg0[:, 0:128], din["emat"]),
            (xg0[:, 128:256], din["mmat"]),
            (xg0[:, 256:256 + C], din["qcT"]),
            (xg0[:, 512:512 + C], din["kcT"]),
            (xg0[:, 768:768 + C], din["vcT"]),
            (xg0[:, 1024:1024 + 2 * NB].rearrange(
                "p (c n) -> p c n", c=2),
             din["oc"].rearrange("(c p) n -> p c n", p=128)),
            (xg1[:, 0:DCK * NB].rearrange("p (c n) -> p c n", c=DCK),
             din["basis"].rearrange("(c p) n -> p c n", p=128)),
            (xg1[:, D:2 * D], din["basisT"]),
            (convw_sb[:, :].rearrange("p (c k) -> p c k", c=HC),
             din["convw"].rearrange("(c p) k -> p c k", p=128)),
            (convb_sb[:, :], din["convb"].rearrange("(c p) -> p c", p=128)),
        ]
        for dst, src in loads:
            E.op("sync", lambda y, d=dst, s=src: y.dma_start(
                out=d, in_=s, allow_slow_non_contiguous=True),
                inc="DC", amt=16)
        dc_all = n["DC"]

        casts = [
            (basis_bf[:, :], xg1[:, 0:DCK * NB]),
            (basisT_bf[:, :], xg1[:, D:2 * D]),
            (qcT_bf[:, :], xg0[:, 256:256 + C]),
            (kcT_bf[:, :], xg0[:, 512:512 + C]),
            (vcT_bf[:, :], xg0[:, 768:768 + C]),
            (oc_bf[:, :], xg0[:, 1024:1024 + 2 * NB]),
            (eye_bf[:, :], eyef[:, :]),
            (eye_h[:, :], eyef[:, :]),
        ]
        for dst, src in casts:
            E.op("vector", lambda v, d=dst, s=src: v.tensor_copy(out=d, in_=s),
                 waits=[("DC", dc_all)])
        if "convdiag" not in DBG_SKIP:
            for idx in range(96):
                E.op("vector", lambda v, i=idx: v.tensor_scalar(
                    out=convdiag[:, i * 32:(i + 1) * 32], in0=e32_sb,
                    scalar1=convw_sb[:, i:i + 1], scalar2=None, op0=ALU.mult),
                    waits=[("DC", dc_all)])

        E.op("vector", lambda v: v.tensor_scalar(
            out=col(28), in0=onec[:, :], scalar1=2.0 ** -30,
            scalar2=1.0, op0=ALU.add, op1=ALU.subtract),
            waits=[("DC", dc_all)])
        E.op("vector", lambda v: v.tensor_scalar(
            out=col(28), in0=col(28), scalar1=2.0 ** 30, scalar2=None,
            op0=ALU.mult))
        E.op("vector", lambda v: v.tensor_scalar(
            out=col(29), in0=col(28), scalar1=-1.0, scalar2=1.0,
            op0=ALU.mult, op1=ALU.add))
        E.op("scalar", lambda s: s.activation(
            out=scal[0:1, 2:3], in_=scal[0:1, 0:1], func=AF.Sigmoid),
            waits=[("DC", dc_all)])
        E.op("scalar", lambda s: s.activation(
            out=scal[0:1, 3:4], in_=scal[0:1, 2:3], func=AF.Ln))
        E.op("scalar", lambda s: s.activation(
            out=scal[0:1, 4:5], in_=scal[0:1, 3:4], func=AF.Exp, scale=128.0))
        s_decay = n["S"]

        for src, slot in [(scal[0:1, 3:4], LND), (scal[0:1, 4:5], DCN),
                          (scal[0:1, 1:2], OSC)]:
            E.op("tensor", lambda t, s2=src, sl=slot: t.matmul(
                out=psG[:, sl:sl + 1], lhsT=onesf[:, :], rhs=s2,
                start=True, stop=True), waits=[("S", s_decay), ("DC", dc_all)])
        t_bc = n["T"]
        for slot in (LND, DCN, OSC):
            E.op("vector", lambda v, sl=slot: v.tensor_copy(
                out=col(sl), in_=psG[:, sl:sl + 1]), waits=[("T", t_bc)])
        v_bc = n["V"]
        setr("psG0", "V")

        E.op("vector", lambda v: v.tensor_scalar(
            out=basisT_bf[:, :], in0=basisT_bf[:, :], scalar1=col(OSC),
            scalar2=None, op0=ALU.mult), waits=[("V", v_bc)])
        E.op("scalar", lambda s: s.activation(
            out=col(QFC), in_=cols[:, 9:10], func=AF.Exp, scale=col(LND)),
            waits=[("V", v_bc)])
        E.op("scalar", lambda s: s.activation(
            out=col(DVC), in_=cols[:, 10:11], func=AF.Exp, scale=col(LND)))
        E.op("scalar", lambda s: s.activation(
            out=dtmask[:, :], in_=xg0[:, 0:128], func=AF.Exp, scale=col(LND)))
        s_exp = n["S"]
        E.op("vector", lambda v: v.tensor_tensor(
            out=dtmask[:, :], in0=dtmask[:, :], in1=xg0[:, 128:256],
            op=ALU.mult), waits=[("S", s_exp)])
        setr("xg0", "V")
        setr("xg1", "V")

        wviews = [xg0.ap(), xg1.ap()]

        def wquant(w_apname, nchunk, width, wq_sb, mean_slot, mc_slot, sw_slot,
                   nelem):
            w_ap = din[w_apname]
            hw_ = width // 2
            E.op("vector", lambda v: v.memset(col(MU), 0.0), waits=use("colsA"))
            nhalf = nchunk * 2
            for ci in range(nhalf):
                dw = "DW%d" % (ci % 2)
                row = ci // 2
                off = (ci % 2) * hw_
                E.op("sync", lambda y, r2=row, o2=off, h2=hw_, c=ci:
                     y.dma_start(
                         out=wviews[c % 2][:, 0:h2],
                         in_=w_ap[r2 * 128:(r2 + 1) * 128, o2:o2 + h2]),
                     waits=use("xg%d" % (ci % 2)), inc=dw, amt=16)
                nseg = hw_ // 128
                E.op("vector", lambda v, c=ci, h2=hw_, a=nseg: v.reduce_sum(
                    out=scrf[:, 64:64 + a],
                    in_=wviews[c % 2][:, 0:h2].rearrange(
                        "p (a q) -> p a q", a=a, q=128),
                    axis=AX.X, apply_absolute_value=True),
                    waits=[(dw, n[dw])] + use("scrf"))
                E.op("vector", lambda v, a=nseg: v.reduce_sum(
                    out=col(SQT), in_=scrf[:, 64:64 + a], axis=AX.X))
                setr("scrf", "V")
                E.op("vector", lambda v: v.tensor_tensor(
                    out=col(MU), in0=col(MU), in1=col(SQT), op=ALU.add))
                setr("xg%d" % (ci % 2), "V")
            vsum = n["V"]
            E.op("tensor", lambda t: t.matmul(
                out=psG[:, 0:1], lhsT=onem[:, :], rhs=col(MU),
                start=True, stop=True), waits=[("V", vsum)] + use("psG0"))
            tsum = n["T"]
            E.op("vector", lambda v: v.tensor_scalar(
                out=col(mc_slot), in0=psG[:, 0:1],
                scalar1=1.0 / nelem, scalar2=1e-5, op0=ALU.mult, op1=ALU.max),
                waits=[("T", tsum)])
            E.op("vector", lambda v: v.reciprocal(
                out=col(sw_slot), in_=col(mc_slot)))
            vmc = n["V"]
            setr("psG0", "V")
            for ci in range(nhalf):
                dw = "DW%d" % (ci % 2)
                row = ci // 2
                off = (ci % 2) * hw_
                E.op("sync", lambda y, r2=row, o2=off, h2=hw_, c=ci:
                     y.dma_start(
                         out=wviews[c % 2][:, 0:h2],
                         in_=w_ap[r2 * 128:(r2 + 1) * 128, o2:o2 + h2]),
                     waits=use("xg%d" % (ci % 2)) + [("V", vmc)],
                     inc=dw, amt=16)
                fv = wviews[ci % 2][:, 0:hw_]
                E.op("vector", lambda v, f=fv: v.tensor_scalar(
                    out=f, in0=f, scalar1=col(sw_slot), scalar2=None,
                    op0=ALU.mult), waits=[(dw, n[dw])])
                for sub in range(hw_ // 512):
                    sl_ = slice(sub * 512, (sub + 1) * 512)
                    E.op("vector", lambda v, f=fv, s3=sl_: v.tensor_scalar(
                        out=scrf[:, 0:512], in0=f[:, s3], scalar1=12582912.0,
                        scalar2=12582912.0, op0=ALU.add, op1=ALU.subtract),
                        waits=use("scrf"))
                    E.op("vector", lambda v, f=fv, s3=sl_: v.tensor_scalar(
                        out=f[:, s3], in0=f[:, s3],
                        scalar1=6755399441055744.0,
                        scalar2=6755399441055744.0,
                        op0=ALU.add, op1=ALU.subtract))
                    E.op("vector", lambda v, f=fv, s3=sl_:
                         v.scalar_tensor_tensor(
                             out=f[:, s3], in0=scrf[:, 0:512], scalar=col(29),
                             in1=f[:, s3], op0=ALU.mult, op1=ALU.add))
                    setr("scrf", "V")
                E.op("vector", lambda v, f=fv: v.tensor_scalar(
                    out=f, in0=f, scalar1=-1.0, scalar2=None, op0=ALU.max))
                E.op("vector", lambda v, f=fv, r2=row, o2=off, h2=hw_:
                     v.tensor_scalar(
                         out=wq_sb[:, r2 * width + o2:r2 * width + o2 + h2],
                         in0=f, scalar1=1.0, scalar2=None, op0=ALU.min))
                setr("xg%d" % (ci % 2), "V")

        if "wq" not in DBG_SKIP:
            wquant("w1t", DCK, H, w1q, 5, W1M, SW1, float(D * H))
            wquant("w2t", HC, D, w2q, 6, W2M, SW2, float(H * D))

        def stageA(bi, g, tl, base):
            cur = g % 2
            xs = xgb[cur][:, tl * D:(tl + 1) * D]
            dxk = "DX%d" % cur
            xw = [(dxk, n[dxk])]
            E.op("vector", lambda v: v.reduce_sum(
                out=col(MU), in_=xs, axis=AX.X), waits=xw + use("colsA"))
            E.op("vector", lambda v: v.tensor_scalar(
                out=col(MU), in0=col(MU), scalar1=1.0 / D, scalar2=None,
                op0=ALU.mult))
            E.op("scalar", lambda s: s.activation(
                out=xn_f[:, :], in_=xs, func=AF.Square),
                waits=xw + use("gx"))
            ssq = n["S"]
            setr("gx", "S")
            E.op("vector", lambda v: v.reduce_sum(
                out=col(SQT), in_=xn_f[:, :], axis=AX.X), waits=[("S", ssq)])
            E.op("vector", lambda v: v.tensor_scalar(
                out=col(SQT), in0=col(SQT), scalar1=1.0 / D, scalar2=None,
                op0=ALU.mult))
            E.op("vector", lambda v: v.tensor_tensor(
                out=col(VAR), in0=col(MU), in1=col(MU), op=ALU.mult))
            E.op("vector", lambda v: v.tensor_tensor(
                out=col(VAR), in0=col(SQT), in1=col(VAR), op=ALU.subtract))
            E.op("vector", lambda v: v.tensor_scalar(
                out=col(VAR), in0=col(VAR), scalar1=1e-5, scalar2=None,
                op0=ALU.add))
            E.op("vector", lambda v: v.tensor_scalar(
                out=col(HS_), in0=col(VAR), scalar1=1.0, scalar2=0.5,
                op0=ALU.add, op1=ALU.mult))
            for _ in range(4):
                E.op("vector", lambda v: v.reciprocal(
                    out=col(HT_), in_=col(HS_)))
                E.op("vector", lambda v: v.tensor_tensor(
                    out=col(HT_), in0=col(VAR), in1=col(HT_), op=ALU.mult))
                E.op("vector", lambda v: v.tensor_tensor(
                    out=col(HS_), in0=col(HS_), in1=col(HT_), op=ALU.add))
                E.op("vector", lambda v: v.tensor_scalar(
                    out=col(HS_), in0=col(HS_), scalar1=0.5, scalar2=None,
                    op0=ALU.mult))
            E.op("vector", lambda v: v.reciprocal(out=col(RSTD), in_=col(HS_)))
            E.op("vector", lambda v: v.tensor_tensor(
                out=col(NMR), in0=col(MU), in1=col(RSTD), op=ALU.mult))
            E.op("vector", lambda v: v.tensor_scalar(
                out=col(NMR), in0=col(NMR), scalar1=-1.0, scalar2=None,
                op0=ALU.mult))
            vln = n["V"]
            E.op("scalar", lambda s: s.activation(
                out=xn_f[:, :], in_=xs, func=AF.Identity, bias=col(NMR),
                scale=col(RSTD)), waits=[("V", vln)])
            sxn = n["S"]
            setr("colsA", "S")
            setr("gx", "S")
            E.op("vector", lambda v: v.reduce_max(
                out=col(M1C), in_=xn_f[:, :], axis=AX.X,
                apply_absolute_value=True), waits=[("S", sxn)])
            E.op("vector", lambda v: v.tensor_scalar(
                out=col(M1C), in0=col(M1C), scalar1=1e-5, scalar2=None,
                op0=ALU.max))
            E.op("vector", lambda v: v.reciprocal(out=col(R1S), in_=col(M1C)))
            E.op("vector", lambda v: v.tensor_scalar(
                out=col(R1S), in0=col(R1S), scalar1=127.0, scalar2=None,
                op0=ALU.mult))
            vr1 = n["V"]
            wxnT = use("xqsT") + [res.get("xqsT_t%d" % tl, ("V", 0))]
            for i in range(DCK):
                sl = transp(xn_f[:, i * 128:(i + 1) * 128], bf=False,
                            waits=[("S", sxn)])
                tc = n["T"]
                E.op("vector", lambda v, s2=sl, i2=i, t2=tl: v.tensor_copy(
                    out=xqsT[:, i2 * GT + t2 * 128: i2 * GT + (t2 + 1) * 128],
                    in_=psG[:, s2 * 128:(s2 + 1) * 128]),
                    waits=[("T", tc)] + (wxnT if i == 0 else []))
                setr("psG%d" % sl, "V")
            vxnT = n["V"]
            res["xnf_rd"] = ("T", n["T"])
            w = use("psH0") + [("V", vxnT)]
            for i in range(DCK):
                E.op("tensor", lambda t, i2=i, t2=tl: t.matmul(
                    out=psH[:, 0:128],
                    lhsT=xqsT[:, i2 * GT + t2 * 128: i2 * GT + (t2 + 1) * 128],
                    rhs=basis_bf[:, i2 * NB:(i2 + 1) * NB],
                    start=(i2 == 0), stop=(i2 == DCK - 1)),
                    waits=(w if i == 0 else []))
            txb = n["T"]
            res["xnT_rd"] = ("T", txb)
            E.op("vector", lambda v: v.tensor_copy(
                out=xb_sb[:, :], in_=psH[:, 0:128]),
                waits=[("T", txb)] + use("xb_sb"))
            setr("psH0", "V")
            E.op("vector", lambda v: v.tensor_scalar(
                out=xn_f[:, :], in0=xn_f[:, :], scalar1=col(R1S),
                scalar2=None, op0=ALU.mult),
                waits=[("V", vr1)] + use("xnf_rd"))
            for hh in range(2):
                hsl = slice(hh * 512, (hh + 1) * 512)
                E.op("vector", lambda v, h3=hsl: v.tensor_scalar(
                    out=scr_f32[:, 0:512], in0=xn_f[:, h3],
                    scalar1=12582912.0, scalar2=12582912.0,
                    op0=ALU.add, op1=ALU.subtract), waits=use("scrf"))
                E.op("vector", lambda v, h3=hsl: v.tensor_scalar(
                    out=xn_f[:, h3], in0=xn_f[:, h3], scalar1=6755399441055744.0,
                    scalar2=6755399441055744.0, op0=ALU.add, op1=ALU.subtract))
                E.op("vector", lambda v, h3=hsl: v.scalar_tensor_tensor(
                    out=xn_f[:, h3], in0=scr_f32[:, 0:512], scalar=col(29),
                    in1=xn_f[:, h3], op0=ALU.mult, op1=ALU.add))
            E.op("vector", lambda v, t2=tl: v.tensor_scalar(
                out=col(22 + t2), in0=col(M1C), scalar1=col(W1M + 0),
                scalar2=1.0 / 127.0, op0=ALU.mult, op1=ALU.mult))
            setr("scrf", "V")
            vxqs = n["V"]
            for i in range(DCK):
                sl = transp(xn_f[:, i * 128:(i + 1) * 128], bf=False,
                            waits=[("V", vxqs)])
                tc = n["T"]
                E.op("vector", lambda v, s2=sl, i2=i, t2=tl: v.tensor_copy(
                    out=xqsT[:, i2 * GT + t2 * 128: i2 * GT + (t2 + 1) * 128],
                    in_=psG[:, s2 * 128:(s2 + 1) * 128]),
                    waits=[("T", tc)] +
                    ([res.get("xnT_rd", ("T", 0))] if i == 0 else []))
                setr("psG%d" % sl, "V")
            res["gx"] = ("T", n["T"])
            setr("xqsT_t%d" % tl, "V")
            vxb = n["V"]
            sl = transp(xb_sb[:, :], bf=True, waits=[("V", vxb)])
            ttr = n["T"]
            res["xb_sb"] = ("T", ttr)
            E.op("vector", lambda v, s2=sl: v.tensor_copy(
                out=xbT_sb[:, :], in_=psG[:, s2 * 128:(s2 + 1) * 128]),
                waits=[("T", ttr)] + use("xbT_sb"))
            vxbt = n["V"]
            setr("psG%d" % sl, "V")
            w = (use("psH1") + use("psH2") + [("V", vxbt)])
            for cc in range(2):
                E.op("tensor", lambda t, c2=cc: t.matmul(
                    out=psH[:, 128 + c2 * 128:256 + c2 * 128],
                    lhsT=qcT_bf[:, c2 * 128:(c2 + 1) * 128], rhs=xbT_sb[:, :],
                    start=True, stop=True), waits=(w if cc == 0 else []))
            E.op("tensor", lambda t: t.matmul(
                out=psH[:, 384:512], lhsT=kcT_bf[:, 0:128], rhs=xbT_sb[:, :],
                start=True, stop=True), waits=use("psH3"))
            kslot = psg_slot()
            E.op("tensor", lambda t, s2=kslot: t.matmul(
                out=psG[:, s2 * 128:(s2 + 1) * 128], lhsT=kcT_bf[:, 128:256],
                rhs=xbT_sb[:, :], start=True, stop=True),
                waits=use("psG%d" % kslot))
            E.op("tensor", lambda t: t.matmul(
                out=psE[:, 0:256], lhsT=xbT_sb[:, :], rhs=vcT_bf[:, :],
                start=True, stop=True), waits=use("psE0") + use("psE1h"))
            tqkv = n["T"]
            res["xbT_sb"] = ("T", tqkv)
            E.op("vector", lambda v: v.tensor_copy(
                out=QT_sb[:, :], in_=psH[:, 128:384]),
                waits=[("T", tqkv)] + use("QT_sb"))
            E.op("vector", lambda v: v.tensor_copy(
                out=KT_sb[:, 0:128], in_=psH[:, 384:512]),
                waits=use("KT_sb"))
            E.op("vector", lambda v, s2=kslot: v.tensor_copy(
                out=KT_sb[:, 128:256], in_=psG[:, s2 * 128:(s2 + 1) * 128]))
            setr("psG%d" % kslot, "V")
            setr("psH1", "V")
            setr("psH2", "V")
            setr("psH3", "V")
            setr("QT_sb", "V")
            setr("KT_sb", "V")
            E.op("scalar", lambda s: s.activation(
                out=V_sb[:, :], in_=psE[:, 0:256], func=AF.Identity),
                waits=[("T", tqkv)] + use("V_sb"))
            setr("psE0", "S")
            setr("V_sb", "S")

        def stageB(bi, g, tl, base):
            vq = res.get("QT_sb", ("V", 0))
            vk = res.get("KT_sb", ("V", 0))
            sv = res.get("V_sb", ("S", 0))
            w = use("psH0") + [vk, vq]
            for cc in range(2):
                E.op("tensor", lambda t, c2=cc: t.matmul(
                    out=psH[:, 0:128],
                    lhsT=KT_sb[:, c2 * 128:(c2 + 1) * 128],
                    rhs=QT_sb[:, c2 * 128:(c2 + 1) * 128],
                    start=(c2 == 0), stop=(c2 == 1)),
                    waits=(w if cc == 0 else []))
            ta = n["T"]
            E.op("vector", lambda v: v.tensor_tensor(
                out=amask[:, :], in0=psH[:, 0:128], in1=dtmask[:, :],
                op=ALU.mult), waits=[("T", ta)] + use("amask"))
            vam = n["V"]
            setr("psH0", "V")
            w = use("psE0") + [("V", vam), sv]
            E.op("tensor", lambda t: t.matmul(
                out=psE[:, 0:256], lhsT=amask[:, :], rhs=V_sb[:, :],
                start=True, stop=True), waits=w)
            res["amask"] = ("T", n["T"])
            w = use("psE1h") + use("S_sb")
            for cc in range(2):
                E.op("tensor", lambda t, c2=cc: t.matmul(
                    out=psE[:, 256:512],
                    lhsT=QT_sb[:, c2 * 128:(c2 + 1) * 128],
                    rhs=S_sb[:, c2 * 256:(c2 + 1) * 256],
                    start=(c2 == 0), stop=(c2 == 1)),
                    waits=(w if cc == 0 else []))
            tr = n["T"]
            res["QT_sb"] = ("T", tr)
            Rg = Rgb[g % 2]
            rk = "Rg%d_t%d" % (g % 2, tl)
            E.op("vector", lambda v, t2=tl, R=Rg: v.tensor_copy(
                out=R[:, t2 * C:(t2 + 1) * C], in_=psE[:, 0:256]),
                waits=[("T", tr)] + use(rk))
            E.op("vector", lambda v, t2=tl, R=Rg: v.scalar_tensor_tensor(
                out=R[:, t2 * C:(t2 + 1) * C], in0=psE[:, 256:512],
                scalar=col(QFC), in1=R[:, t2 * C:(t2 + 1) * C],
                op0=ALU.mult, op1=ALU.add))
            setr(rk, "V")
            setr("psE0", "V")
            setr("psE1h", "V")
            for i in range(2):
                sl = transp(KT_sb[:, i * 128:(i + 1) * 128], bf=True,
                            waits=[vk])
                tk = n["T"]
                E.op("scalar", lambda s, s2=sl, i2=i: s.activation(
                    out=Ktm[:, i2 * 128:(i2 + 1) * 128],
                    in_=psG[:, s2 * 128:(s2 + 1) * 128], func=AF.Identity,
                    scale=col(DVC)), waits=[("T", tk)] + use("Ktm"))
                setr("psG%d" % sl, "S")
            res["KT_sb"] = ("T", tk)
            skt = n["S"]
            setr("Ktm", "S")
            w = use("psC0") + use("psC1") + [("S", skt), sv]
            for cc in range(2):
                E.op("tensor", lambda t, c2=cc: t.matmul(
                    out=psC[:, c2 * 256:(c2 + 1) * 256],
                    lhsT=Ktm[:, c2 * 128:(c2 + 1) * 128], rhs=V_sb[:, :],
                    start=True, stop=True), waits=(w if cc == 0 else []))
            tkv = n["T"]
            res["Ktm"] = ("T", tkv)
            res["KT_sb"] = ("T", tkv)
            res["V_sb"] = ("T", tkv)
            E.op("vector", lambda v: v.scalar_tensor_tensor(
                out=S_sb[:, :], in0=S_sb[:, :], scalar=col(DCN),
                in1=psC[:, :], op0=ALU.mult, op1=ALU.add),
                waits=[("T", tkv)])
            setr("S_sb", "V")
            setr("psC0", "V")
            setr("psC1", "V")

        def stageC(g):
            cur = g % 2
            wfirst = ([res.get("xqsT_t0", ("V", 0)),
                       res.get("xqsT_t1", ("V", 0))] + hwaits(cur))
            wev = hwaits(cur)
            for hc in range(HC):
                pb = (hc // 2) % 2
                slot = (hc % 2) * 256
                w = use("psAB%d" % pb) + (wfirst if hc == 0 else [])
                for i in range(DCK):
                    E.op("tensor",
                         lambda t, i2=i, h2=hc, p2=pb, s2=slot: t.matmul(
                             out=psAB[p2][:, s2:s2 + 256],
                             lhsT=w1q[:, i2 * H + h2 * 128:
                                      i2 * H + (h2 + 1) * 128],
                             rhs=xqsT[:, i2 * GT:(i2 + 1) * GT],
                             start=(i2 == 0), stop=(i2 == DCK - 1)),
                         waits=(w if i == 0 else []))
                tmm = n["T"]
                if hc == 0:
                    htouch(cur, "T")
                if hc % 2 == 1:
                    pr = hc // 2
                    dsl = hap(cur, hc - 1, 2)
                    ssl = psAB[pb][:, 0:512].rearrange(
                        "p (a q) -> p a q", a=2, q=256)
                    scb = scol_sb.ap().unsqueeze(1).broadcast_to(
                        [128, 2, 256])
                    E.op("vector", lambda v, d=dsl, s3=ssl, s4=scb:
                         v.tensor_tensor(out=d, in0=s3, in1=s4, op=ALU.mult),
                         waits=[("T", tmm),
                                res.get("scol_copy", ("V", 0))] +
                         (wev if pr == 0 else []))
                    setr("psAB%d" % pb, "V")
            res["xqsT"] = ("T", n["T"])
            setr("scol_sb", "V")
            setr("psH2", "V")
            setr("psH3", "V")
            htouch(cur, "V")
            htouch(cur, "S")

        def tail(bi, k, base):
            kb = k % 2
            hk = hsb[kb]
            wh = hwaits(kb)
            for hc in range(HC):
                pb = hc % 2
                w = use("psCD%d" % pb) + (wh if hc == 0 else [])
                for i in range(4):
                    for kk in range(3):
                        E.op("tensor",
                             lambda t, h2=hc, k2=kk, i2=i, p2=pb:
                             t.matmul(
                                 out=psCD[p2][32 * i2:32 * (i2 + 1), 0:256],
                                 lhsT=convdiag[32 * i2:32 * (i2 + 1),
                                               (h2 * 3 + k2) * 32:
                                               (h2 * 3 + k2 + 1) * 32],
                                 rhs=hk[32 * i2:32 * (i2 + 1),
                                        h2 * HSW + k2:h2 * HSW + k2 + 256],
                                 start=(k2 == 0), stop=(k2 == 2),
                                 tile_position=(32 * i2, 32 * i2)),
                             waits=(w if (kk == 0 and i == 0) else []))
                tcv = n["T"]
                E.op("scalar", lambda s, h2=hc, p2=pb: s.activation(
                    out=hk[:, h2 * HSW + 1:h2 * HSW + 1 + GT],
                    in_=psCD[p2][:, 0:256], func=AF.Silu,
                    bias=convb_sb[:, h2:h2 + 1]),
                    waits=[("T", tcv)] + (wh if hc <= 1 else []),
                    selfwait=False)
                setr("psCD%d" % pb, "S")
            htouch(kb, "T")
            sconv = n["S"]
            htouch(kb, "S")
            wtree = [("S", sconv)] + use("scr") + use("scr_rt")
            for op2, base2 in ((ALU.max, 0), (ALU.min, 1)):
                E.op("vector", lambda v, o3=op2: v.tensor_tensor(
                    out=scr[:, 0:512].rearrange("p (a q) -> p a q", a=2, q=GT),
                    in0=hap(kb, 0, 2), in1=hap(kb, 2, 2), op=o3),
                    waits=(wtree if base2 == 0 else []))
                for c0 in range(4, HC, 2):
                    E.op("vector", lambda v, c2=c0, o3=op2: v.tensor_tensor(
                        out=scr[:, 0:512].rearrange(
                            "p (a q) -> p a q", a=2, q=GT),
                        in0=scr[:, 0:512].rearrange(
                            "p (a q) -> p a q", a=2, q=GT),
                        in1=hap(kb, c2, 2), op=o3))
                E.op("vector", lambda v, o3=op2: v.tensor_tensor(
                    out=scr[:, 0:256], in0=scr[:, 0:256], in1=scr[:, 256:512],
                    op=o3))
                vtree = n["V"]
                for tl in range(2):
                    sl = transp(scr[:, tl * 128:(tl + 1) * 128], bf="f16",
                                waits=[("V", vtree)])
                    ttr = n["T"]
                    if base2 == 1 and tl == 1:
                        res["scr_rt"] = ("T", ttr)
                    if base2 == 0:
                        E.op("vector", lambda v, s2=sl, t2=tl: v.reduce_max(
                            out=col(24 + t2),
                            in_=psG[:, s2 * 128:(s2 + 1) * 128], axis=AX.X),
                            waits=[("T", ttr)] + use("colsB"))
                    else:
                        E.op("vector", lambda v, s2=sl, t2=tl: v.tensor_reduce(
                            out=col(M2C),
                            in_=psG[:, s2 * 128:(s2 + 1) * 128], axis=AX.X,
                            op=ALU.min), waits=[("T", ttr)] + use("colsB"))
                        E.op("vector", lambda v, t2=tl: v.tensor_scalar(
                            out=col(M2C), in0=col(M2C), scalar1=-1.0,
                            scalar2=None, op0=ALU.mult))
                        E.op("vector", lambda v, t2=tl: v.tensor_tensor(
                            out=col(M2C), in0=col(M2C), in1=col(24 + t2),
                            op=ALU.max))
                        E.op("vector", lambda v: v.tensor_scalar(
                            out=col(M2C), in0=col(M2C), scalar1=1e-5,
                            scalar2=None, op0=ALU.max))
                        E.op("vector", lambda v, t2=tl: v.tensor_scalar(
                            out=col(SCL0 + t2), in0=col(M2C),
                            scalar1=col(W2M + 0), scalar2=1.0 / 127.0,
                            op0=ALU.mult, op1=ALU.mult))
                        E.op("vector", lambda v, t2=tl: v.reciprocal(
                            out=col(26 + t2), in_=col(M2C)))
                        E.op("vector", lambda v, t2=tl: v.tensor_scalar(
                            out=col(26 + t2), in0=col(26 + t2), scalar1=127.0,
                            scalar2=None, op0=ALU.mult))
                    setr("psG%d" % sl, "V")
                if base2 == 0:
                    setr("scr", "V")
            setr("colsB", "V")
            vrbcols = n["V"]
            slb = psg_slot2()
            for t2 in range(2):
                E.op("vector", lambda v, t3=t2: v.tensor_scalar(
                    out=dgt[:, :], in0=eyef[:, :], scalar1=col(26 + t3),
                    scalar2=None, op0=ALU.mult),
                    waits=[("V", vrbcols)] + use("dgt"))
                vdg2 = n["V"]
                E.op("tensor", lambda t, t3=t2, s2=slb: t.matmul(
                    out=psG[:, (s2 + t3) * 128:(s2 + t3 + 1) * 128],
                    lhsT=onem[:, :], rhs=dgt[:, :], start=True, stop=True),
                    waits=[("V", vdg2)] + use("psG%d" % (slb + t2)))
                res["dgt"] = ("T", n["T"])
            tbc = n["T"]
            rb_ps = psG[:, slb * 128:slb * 128 + 256]
            wrd = [("T", tbc)] + use("scrf") + use("gx")
            for q in range(HC // 2):
                c0 = q * 2
                E.op("vector", lambda v, c2=c0, rp=rb_ps: v.tensor_tensor(
                    out=scr_f32[:, 0:512].rearrange(
                        "p (a q2) -> p a q2", a=2, q2=GT),
                    in0=hap(kb, c2, 2),
                    in1=rp.unsqueeze(1).broadcast_to([128, 2, GT]),
                    op=ALU.mult), waits=(wrd if q == 0 else []))
                E.op("vector", lambda v: v.tensor_scalar(
                    out=g_sb[:, 0:512], in0=scr_f32[:, 0:512],
                    scalar1=12582912.0, scalar2=12582912.0,
                    op0=ALU.add, op1=ALU.subtract))
                E.op("vector", lambda v: v.tensor_scalar(
                    out=scr_f32[:, 0:512], in0=scr_f32[:, 0:512],
                    scalar1=6755399441055744.0, scalar2=6755399441055744.0,
                    op0=ALU.add, op1=ALU.subtract))
                E.op("vector", lambda v, c2=c0: v.scalar_tensor_tensor(
                    out=hap(kb, c2, 2),
                    in0=g_sb[:, 0:512].rearrange(
                        "p (a q2) -> p a q2", a=2, q2=GT),
                    scalar=col(29),
                    in1=scr_f32[:, 0:512].rearrange(
                        "p (a q2) -> p a q2", a=2, q2=GT),
                    op0=ALU.mult, op1=ALU.add))
            res["gx"] = ("V", n["V"])
            setr("scrf", "V")
            vhq = n["V"]
            setr("psG%d" % slb, "V")
            setr("psG%d" % (slb + 1), "V")
            htouch(kb, "V")
            setr("scr", "V")
            Rg = Rgb[kb]
            xk = xgb[kb]
            dxk = "DX%d" % kb
            for tl in range(2):
                w = (use("psE0") + use("psE1h") + use("psF") + [("V", vhq)] +
                     [res.get("colsB", ("V", 0))])
                for hc in range(HC):
                    for dgrp, p2 in ((0, psE), (1, psF)):
                        E.op("tensor",
                             lambda t, h2=hc, d2=dgrp, t2=tl, p3=p2: t.matmul(
                                 out=p3[:, 0:512],
                                 lhsT=hk[:, h2 * HSW + 1 + t2 * 128:
                                         h2 * HSW + 1 + (t2 + 1) * 128],
                                 rhs=w2q[:, h2 * D + d2 * 512:
                                         h2 * D + (d2 + 1) * 512],
                                 start=(h2 == 0), stop=(h2 == HC - 1)),
                             waits=(w if (hc == 0 and dgrp == 0) else []))
                tf2 = n["T"]
                htouch(kb, "T")
                for dgrp, p2 in ((0, psE), (1, psF)):
                    E.op("scalar", lambda s, d2=dgrp, p3=p2, t2=tl:
                         s.activation(
                             out=g_sb[:, d2 * 512:(d2 + 1) * 512],
                             in_=p3[:, 0:512], func=AF.Silu,
                             scale=col(SCL0 + t2)),
                         waits=[("T", tf2), ("V", vhq)] +
                         (use("gx") if dgrp == 0 else []),
                         selfwait=False)
                sg_ = n["S"]
                setr("gx", "S")
                res["colsB"] = ("S", sg_)
                setr("psE0", "S")
                setr("psE1h", "S")
                setr("psF", "S")
                rgw = [res.get("Rg%d_t%d" % (kb, tl), ("V", 0))]
                for i in range(2):
                    sl = transp(Rg[:, tl * C + i * 128:tl * C + (i + 1) * 128],
                                bf=True, waits=rgw)
                    trt = n["T"]
                    E.op("vector", lambda v, s2=sl, i2=i, t2=tl, R=Rg:
                         v.tensor_copy(
                             out=R[:, t2 * C + i2 * 128:
                                   t2 * C + (i2 + 1) * 128],
                             in_=psG[:, s2 * 128:(s2 + 1) * 128]),
                         waits=[("T", trt)])
                    setr("psG%d" % sl, "V")
                vrt = n["V"]
                slu = psg_slot()
                w = use("psG%d" % slu) + [("V", vrt)]
                for cc in range(2):
                    E.op("tensor", lambda t, c2=cc, s2=slu, t2=tl, R=Rg:
                         t.matmul(
                             out=psG[:, s2 * 128:(s2 + 1) * 128],
                             lhsT=R[:, t2 * C + c2 * 128:
                                    t2 * C + (c2 + 1) * 128],
                             rhs=oc_bf[:, c2 * 128:(c2 + 1) * 128],
                             start=(c2 == 0), stop=(c2 == 1)),
                         waits=(w if cc == 0 else []))
                tu = n["T"]
                res["Rg%d_t%d" % (kb, tl)] = ("T", tu)
                E.op("vector", lambda v, s2=slu: v.tensor_copy(
                    out=u_sb[:, :], in_=psG[:, s2 * 128:(s2 + 1) * 128]),
                    waits=[("T", tu)] + use("u_sb"))
                setr("psG%d" % slu, "V")
                vu_ = n["V"]
                slut = transp(u_sb[:, :], bf=True, waits=[("V", vu_)])
                tut = n["T"]
                res["u_sb"] = ("T", tut)
                E.op("vector", lambda v, s2=slut: v.tensor_copy(
                    out=uT_sb[:, :], in_=psG[:, s2 * 128:(s2 + 1) * 128]),
                    waits=[("T", tut)] + use("uT_sb"))
                vut = n["V"]
                setr("psG%d" % slut, "V")
                w = use("psE0") + use("psE1h") + use("psF") + [("V", vut)]
                for dgrp, p2 in ((0, psE), (1, psF)):
                    E.op("tensor", lambda t, d2=dgrp, p3=p2: t.matmul(
                        out=p3[:, 0:512], lhsT=uT_sb[:, :],
                        rhs=basisT_bf[:, d2 * 512:(d2 + 1) * 512],
                        start=True, stop=True), waits=(w if dgrp == 0 else []))
                ts_ = n["T"]
                res["uT_sb"] = ("T", ts_)
                E.op("vector", lambda v: v.tensor_tensor(
                    out=g_sb[:, 0:512], in0=g_sb[:, 0:512], in1=psE[:, 0:512],
                    op=ALU.mult), waits=[("T", ts_), ("S", sg_)])
                E.op("vector", lambda v: v.tensor_tensor(
                    out=g_sb[:, 512:1024], in0=g_sb[:, 512:1024],
                    in1=psF[:, 0:512], op=ALU.mult))
                setr("psE0", "V")
                setr("psE1h", "V")
                setr("psF", "V")
                E.op("vector", lambda v, t2=tl, xk2=xk: v.tensor_tensor(
                    out=xk2[:, t2 * D:(t2 + 1) * D],
                    in0=xk2[:, t2 * D:(t2 + 1) * D], in1=g_sb[:, :],
                    op=ALU.add), waits=[(dxk, n[dxk])])
                vcb = n["V"]
                setr("gx", "V")
                r0 = base + k * GT + tl * 128
                do = "DO%d" % kb
                E.op("gpsimd", lambda gp, r=r0, t2=tl, xk2=xk: gp.dma_start(
                    out=out_ap[r:r + 128, :], in_=xk2[:, t2 * D:(t2 + 1) * D]),
                    waits=[("V", vcb)], inc=do, amt=16)
            res["xg%d" % kb] = ("DO%d" % kb, n["DO%d" % kb])

        for bi in range(nbatch):
            base = bi * S
            E.op("vector", lambda v: v.memset(S_sb[:, :], 0.0),
                 waits=use("S_sb"))
            setr("S_sb", "V")
            hi = (NG - 1) % 2
            E.op("vector", lambda v, b=hi: v.memset(
                hsb[b][:, GT + 1:HC * HSW:HSW], 0.0), waits=hwaits(hi))
            htouch(hi, "V")
            for g0 in (NG - 1, NG - 2):
                if g0 < 0:
                    continue
                bidx = g0 % 2
                dx = "DX%d" % bidx
                r0 = base + g0 * GT
                E.op("sync", lambda y, b=bidx, r=r0: y.dma_start(
                    out=xgb[b].ap().rearrange("p (a d) -> p a d", a=2),
                    in_=din["x"][r:r + GT, :].rearrange(
                        "(a p) d -> p a d", p=128)),
                    waits=use("xg%d" % bidx), inc=dx, amt=16)
                setr("xg%d" % bidx, dx)

            for g in range(NG - 1, -1, -1):
                cur = g % 2
                for tl in (1, 0):
                    if "A" not in DBG_SKIP:
                        stageA(bi, g, tl, base)
                    if "B" not in DBG_SKIP:
                        stageB(bi, g, tl, base)
                if dbg_ap is not None and g == NG - 1 and bi == 0:
                    wv = [("V", n["V"]), ("S", n["S"]), ("T", n["T"])]
                    E.op("gpsimd", lambda gp: gp.dma_start(
                        out=dbg_ap[:, 0:10], in_=cols[:, 0:10]), waits=wv,
                        inc="DO0", amt=16)
                    E.op("gpsimd", lambda gp: gp.dma_start(
                        out=dbgb_ap[:, 0:256], in_=basis_bf[:, 0:256]),
                        inc="DO0", amt=16)
                    E.op("gpsimd", lambda gp: gp.dma_start(
                        out=dbgb_ap[:, 256:512], in_=xqsT[:, 0:256]),
                        inc="DO0", amt=16)
                    E.op("gpsimd", lambda gp: gp.dma_start(
                        out=dbgb_ap[:, 512:768], in_=QT_sb[:, 0:256]),
                        inc="DO0", amt=16)
                    E.op("gpsimd", lambda gp: gp.dma_start(
                        out=dbgb_ap[:, 768:1024], in_=S_sb[:, 0:256]),
                        inc="DO0", amt=16)
                    E.op("gpsimd", lambda gp: gp.dma_start(
                        out=dbgh_ap[:, 0:512], in_=w2q[:, 0:512]),
                        inc="DO0", amt=16)
                if "SCOL" not in DBG_SKIP:
                    vsc = n["V"]
                    for tl in range(2):
                        E.op("vector", lambda v, t2=tl: v.tensor_scalar(
                            out=dgt[:, :], in0=eyef[:, :],
                            scalar1=col(22 + t2), scalar2=None, op0=ALU.mult),
                            waits=use("dgt"))
                        vdg = n["V"]
                        E.op("tensor", lambda t, t2=tl: t.matmul(
                            out=psH[:, 256 + t2 * 128:384 + t2 * 128],
                            lhsT=onem[:, :], rhs=dgt[:, :],
                            start=True, stop=True),
                            waits=[("V", vdg)] +
                            (use("psH2") + use("psH3") if tl == 0 else []))
                        res["dgt"] = ("T", n["T"])
                    tscb = n["T"]
                    E.op("vector", lambda v: v.tensor_copy(
                        out=scol_sb[:, :], in_=psH[:, 256:512]),
                        waits=[("T", tscb)] + use("scol_sb"))
                    vscb = n["V"]
                    res["scol_copy"] = ("V", vscb)
                if "C" not in DBG_SKIP:
                    stageC(g)
                if g <= NG - 2:
                    oth = (g + 1) % 2
                    wf = hwaits(cur) + hwaits(oth)
                    E.op("vector", lambda v, o=oth, c=cur: v.tensor_copy(
                        out=hsb[o][:, 0:HC * HSW:HSW],
                        in_=hsb[c][:, GT:HC * HSW:HSW]), waits=wf)
                    E.op("vector", lambda v, o=oth, c=cur: v.tensor_copy(
                        out=hsb[c][:, GT + 1:HC * HSW:HSW],
                        in_=hsb[o][:, 1:HC * HSW:HSW]))
                    htouch(cur, "V")
                    htouch(oth, "V")
                    if "TAIL" not in DBG_SKIP:
                        tail(bi, g + 1, base)
                if 1 <= g <= NG - 2:
                    gnext = g - 1
                    bidx = gnext % 2
                    dx = "DX%d" % bidx
                    r0 = base + gnext * GT
                    E.op("sync", lambda y, b=bidx, r=r0: y.dma_start(
                        out=xgb[b].ap().rearrange("p (a d) -> p a d", a=2),
                        in_=din["x"][r:r + GT, :].rearrange(
                            "(a p) d -> p a d", p=128)),
                        waits=use("xg%d" % bidx), inc=dx, amt=16)
                    setr("xg%d" % bidx, dx)
            E.op("vector", lambda v: v.memset(hs0[:, 0:HC * HSW:HSW], 0.0),
                 waits=hwaits(0))
            htouch(0, "V")
            if "TAIL" not in DBG_SKIP:
                tail(bi, 0, base)

        E.barrier_wait("gpsimd", [("DO0", n["DO0"]), ("DO1", n["DO1"])])

    with ctx:
        with nc.Block() as blk:
            for engname, reg in (("sync", blk.sync), ("tensor", blk.tensor),
                                 ("vector", blk.vector),
                                 ("scalar", blk.scalar),
                                 ("gpsimd", blk.gpsimd)):
                def body(eng, engname=engname):
                    schedule(Emit(engname, eng, sems, strict=strict,
                                  limit=limit))
                reg(body)
    return nc


def kernel(L0_attn_x, L0_attn_basis, L0_attn_q_coeffs, L0_attn_k_coeffs,
           L0_attn_v_coeffs, L0_attn_o_coeffs, L0_attn_decay_logit,
           L0_attn_out_scale, L0_ffn_w1, L0_ffn_conv_w, L0_ffn_conv_b,
           L0_ffn_w2):
    x = np.ascontiguousarray(np.asarray(L0_attn_x, dtype=np.float32))
    basis = np.ascontiguousarray(np.asarray(L0_attn_basis, np.float32))
    com = dict(
        basis=basis,
        basisT=np.ascontiguousarray(basis.T),
        qcT=np.ascontiguousarray(np.asarray(L0_attn_q_coeffs, np.float32).T),
        kcT=np.ascontiguousarray(np.asarray(L0_attn_k_coeffs, np.float32).T),
        vcT=np.ascontiguousarray(np.asarray(L0_attn_v_coeffs, np.float32).T),
        oc=np.ascontiguousarray(np.asarray(L0_attn_o_coeffs, np.float32)),
        w1t=np.ascontiguousarray(np.asarray(L0_ffn_w1, np.float32).T),
        w2t=np.ascontiguousarray(np.asarray(L0_ffn_w2, np.float32).T),
        convw=np.ascontiguousarray(
            np.asarray(L0_ffn_conv_w, np.float32).reshape(H, 3)),
        convb=np.ascontiguousarray(np.asarray(L0_ffn_conv_b, np.float32)),
        dlogit=np.asarray(L0_attn_decay_logit, np.float32).reshape(1, 1),
        oscale=np.asarray(L0_attn_out_scale, np.float32).reshape(1, 1),
    )
    com.update(host_consts())
    nc = build_nc()
    in_maps = []
    for c in range(NCORES):
        shard = np.ascontiguousarray(
            x[c * NB_BATCH:(c + 1) * NB_BATCH].reshape(NB_BATCH * S_FULL, D))
        m = {"x": shard}
        m.update(com)
        in_maps.append(m)
    res = run_bass_kernel_spmd(nc, in_maps, core_ids=list(range(NCORES)))
    outs = [np.asarray(r["out"], np.float32).reshape(NB_BATCH, S_FULL, D)
            for r in res.results]
    return np.concatenate(outs, axis=0)
